# version 41
# baseline (speedup 1.0000x reference)
import math
import numpy as np
import concourse.bass as bass
import concourse.mybir as mybir
from concourse.bass_utils import run_bass_kernel_spmd

F32 = mybir.dt.float32
BF16 = mybir.dt.bfloat16
AF = mybir.ActivationFunctionType
AOP = mybir.AluOpType

S, D, H, HD = 4096, 1024, 16, 64
NT = S // 128
NF = D // 128
SC = 1024
NSC = S // SC
TC = 512
NTC = S // TC
EPS = 1e-5
MARGIN = 3.0
SKIP_T = 65.0
NCORES = 8

SLOPES = [2.0 ** (-(h + 1) / 2.0) for h in range(H)]
N0 = [max(0, math.ceil((S - 128 - SKIP_T / SLOPES[h]) / 128.0)) for h in range(H)]
N0_PAIR = [min(N0[2 * p], N0[2 * p + 1]) for p in range(H // 2)]

KIB = 1024


class Sem:
    def __init__(self, nc, name):
        self.h = nc.alloc_semaphore(name)
        self.n = 0

    def inc(self, instr, k=1):
        instr.then_inc(self.h, k)
        self.n += k

    def wait(self, eng, v=None):
        v = self.n if v is None else v
        if v > 0:
            eng.wait_ge(self.h, v)


def build_program(debug=False):
    nc = bass.Bass("TRN2", target_bir_lowering=False)

    x_h = nc.dram_tensor("x", [S, D], F32, kind="ExternalInput").ap()
    wkqvT = nc.dram_tensor("wkqvT", [D, 3 * D], F32, kind="ExternalInput").ap()
    w1r = nc.dram_tensor("w1r", [NF, D, 128], F32, kind="ExternalInput").ap()
    w2r = nc.dram_tensor("w2r", [4, D, 128], F32, kind="ExternalInput").ap()
    cw3 = nc.dram_tensor("cw3", [512, 256], F32, kind="ExternalInput").ap()
    cw4 = nc.dram_tensor("cw4", [128, 2], F32, kind="ExternalInput").ap()
    b1c_h = nc.dram_tensor("b1c", [128, 8], F32, kind="ExternalInput").ap()
    b2c_h = nc.dram_tensor("b2c", [128, 4], F32, kind="ExternalInput").ap()
    b3c_h = nc.dram_tensor("b3c", [128, 2], F32, kind="ExternalInput").ap()
    b4_h = nc.dram_tensor("b4", [1, 1], F32, kind="ExternalInput").ap()
    gw1 = nc.dram_tensor("gw1", [D, 512], F32, kind="ExternalInput").ap()
    gw2 = nc.dram_tensor("gw2", [512, 256], F32, kind="ExternalInput").ap()
    gw3 = nc.dram_tensor("gw3", [128, 2], F32, kind="ExternalInput").ap()
    gb1_h = nc.dram_tensor("gb1", [1, 512], F32, kind="ExternalInput").ap()
    gb2_h = nc.dram_tensor("gb2", [1, 256], F32, kind="ExternalInput").ap()
    gb3_h = nc.dram_tensor("gb3", [1, 1], F32, kind="ExternalInput").ap()
    ar_h = nc.dram_tensor("ar", [1, 1], F32, kind="ExternalInput").ap()
    identf_h = nc.dram_tensor("identf", [128, 128], F32, kind="ExternalInput").ap()
    onesc_h = nc.dram_tensor("onesc", [1, 128], F32, kind="ExternalInput").ap()
    onesb_h = nc.dram_tensor("onesb", [1, 128], BF16, kind="ExternalInput").ap()
    ebias_h = nc.dram_tensor("ebias", [128, 512], F32, kind="ExternalInput").ap()
    sel_h = nc.dram_tensor("sel", [16, 1024], F32, kind="ExternalInput").ap()
    cv_h = nc.dram_tensor("cv", [128, 4], F32, kind="ExternalInput").ap()

    out_h = nc.dram_tensor("out", [S, D], F32, kind="ExternalOutput").ap()

    sk = "ExternalOutput" if debug else "Internal"
    w1b_s = nc.dram_tensor("w1b_s", [NF, D, 128], BF16, kind="Internal").ap()
    w2b_s = nc.dram_tensor("w2b_s", [4, D, 128], BF16, kind="Internal").ap()
    sraw_s = nc.dram_tensor("sraw_s", [D, S], F32, kind=sk).ap()
    d_s = nc.dram_tensor("d_s", [16, S], F32, kind=sk).ap()

    identf = nc.alloc_sbuf_tensor("identf_s", [128, 128], F32).ap()
    onesc = nc.alloc_sbuf_tensor("onesc_s", [1, 128], F32).ap()
    onesb = nc.alloc_sbuf_tensor("onesb_s", [1, 128], BF16).ap()
    ebias = nc.alloc_sbuf_tensor("ebias_s", [128, 512], F32).ap()
    sel = nc.alloc_sbuf_tensor("sel_s", [16, 1024], F32).ap()
    rcol = nc.alloc_sbuf_tensor("rcol", [128, 32], F32).ap()
    nmr = nc.alloc_sbuf_tensor("nmr", [128, 32], F32).ap()
    stat = nc.alloc_sbuf_tensor("stat", [128, 48], F32).ap()
    mvs = nc.alloc_sbuf_tensor("mvs", [128, 8], F32).ap()
    lt4 = nc.alloc_sbuf_tensor("lt4", [128, 4], F32).ap()
    gct = nc.alloc_sbuf_tensor("gct", [128, 8], F32).ap()
    gctb = nc.alloc_sbuf_tensor("gctb", [128, 8], BF16).ap()
    b1c = nc.alloc_sbuf_tensor("b1c_s", [128, 8], F32).ap()
    b2c = nc.alloc_sbuf_tensor("b2c_s", [128, 4], F32).ap()
    b3c = nc.alloc_sbuf_tensor("b3c_s", [128, 2], F32).ap()
    b4 = nc.alloc_sbuf_tensor("b4_s", [1, 1], F32).ap()
    b4h = nc.alloc_sbuf_tensor("b4h", [1, 1], F32).ap()
    gb1 = nc.alloc_sbuf_tensor("gb1_s", [1, 512], F32).ap()
    gb2 = nc.alloc_sbuf_tensor("gb2_s", [1, 256], F32).ap()
    gb3 = nc.alloc_sbuf_tensor("gb3_s", [1, 1], F32).ap()
    gb3hf = nc.alloc_sbuf_tensor("gb3hf", [1, 1], F32).ap()
    ar = nc.alloc_sbuf_tensor("ar_s", [1, 1], F32).ap()
    thg = nc.alloc_sbuf_tensor("thg", [1, 1], F32).ap()
    bp0 = nc.alloc_sbuf_tensor("bp0", [1, 1], F32).ap()
    bp = nc.alloc_sbuf_tensor("bp", [1, 1], F32).ap()
    ap_ = nc.alloc_sbuf_tensor("ap_", [1, 1], F32).ap()
    w3sb = nc.alloc_sbuf_tensor("w3sb", [128, 4 * 256], BF16).ap()
    w4sb = nc.alloc_sbuf_tensor("w4sb", [128, 2], BF16).ap()
    gw3sb = nc.alloc_sbuf_tensor("gw3sb", [128, 2], BF16).ap()
    thr = nc.alloc_sbuf_tensor("thr", [1, 512], F32).ap()
    cr = nc.alloc_sbuf_tensor("cr", [1, 512], F32).ap()
    g1b = nc.alloc_sbuf_tensor("g1b", [1, 512], BF16).ap()
    g1c = nc.alloc_sbuf_tensor("g1c", [128, 4], BF16).ap()
    gr2 = nc.alloc_sbuf_tensor("gr2", [1, 256], F32).ap()
    cv = nc.alloc_sbuf_tensor("cv_s", [128, 4], F32).ap()
    g2b = nc.alloc_sbuf_tensor("g2b", [1, 256], BF16).ap()
    g2c = nc.alloc_sbuf_tensor("g2c", [128, 2], BF16).ap()

    A0 = (nc.sbuf_base + 63) & ~63
    TOP = 229368 - 8
    assert A0 + 186 * KIB <= TOP, f"arena too small: {A0}"

    def at(name, shape, dtype, off_kib):
        t = nc.alloc_sbuf_tensor_at(name, shape, dtype, offset=A0 + int(off_kib * KIB))
        return t.ap()

    XB = at("XB", [128, NF * S], BF16, 0)
    WK = at("WK", [128, NF * 3 * D], BF16, 64)
    XTs = [at(f"XTs{i}", [128, 4 * D], F32, 112 + 16 * i) for i in range(3)]
    XNF = [at(f"XNF{i}", [128, D], F32, 160 + 4 * i) for i in range(4)]
    KT = [at(f"KT{i}", [128, S], BF16, 112 + 8 * i) for i in range(2)]
    QT = [at(f"QT{i}", [128, S], BF16, 128 + 8 * i) for i in range(2)]
    VB = [at(f"VB{i}", [128, 32 * 130], BF16, 144 + 8.25 * i) for i in range(2)]
    PB_ = [at(f"PBUF{i}", [128, 512], BF16, 160.5 + 1 * i) for i in range(12)]
    SRW = [at(f"SRW{i}", [128, 1024], F32, 172.5 + 4 * i) for i in range(2)]
    STATE = at("STATE", [128, NF * S], F32, 0)
    XT2s = [at(f"XT2s{i}", [128, 2 * D], F32, 128 + 8 * i) for i in range(2)]
    XNF2 = [at(f"XNF2{i}", [128, D], F32, 144 + 4 * i) for i in range(4)]
    GCW1 = at("GCW1", [128, NF * 512], BF16, 160)
    GCW2 = at("GCW2", [128, 4 * 256], BF16, 168)
    SBF2 = [at(f"SBF{i}", [128, NF * TC], BF16, 128 + 8 * i) for i in range(2)]
    H12 = [at(f"H1{i}", [128, NF * TC], BF16, 144 + 8 * i) for i in range(2)]
    H2 = at("H2", [128, 4 * TC], BF16, 160)
    H3 = at("H3", [128, 2 * TC], BF16, 164)
    W2S = [at(f"W2S{i}", [128, NF * 128], BF16, 166 + 2 * i) for i in range(2)]
    W1S = [at(f"W1S{i}", [128, NF * 128], BF16, 170 + 2 * i) for i in range(4)]
    TMP = [at(f"TMP{i}", [128, 512], F32, 178 + 2 * i) for i in range(2)]
    TCF = 256
    SINB = [at(f"SINB{i}", [128, NF * TCF], F32, 128 + 8 * i) for i in range(2)]
    FTMP = [at(f"FTMP{i}", [128, TCF], F32, 144 + 1 * i) for i in range(2)]
    RD = at("RD", [16, S], F32, 146)
    XT3 = [at(f"XT3{i}", [128, D], F32, 162 + 4 * i) for i in range(2)]
    OUT = [at(f"OUT{i}", [128, D], F32, 170 + 4 * i) for i in range(2)]
    SG2 = [at(f"SG{i}", [128, NF * TCF], F32, 178) for i in range(1)]

    v4 = [VB[i].rearrange("p (nt h d) -> p nt h d", nt=32, h=2) for i in range(2)]

    PA = nc.alloc_psum_tensor("PA", [128, 1024], F32).ap()
    PBp = nc.alloc_psum_tensor("PBp", [128, 1024], F32).ap()
    PC = nc.alloc_psum_tensor("PC", [128, 1024], F32).ap()
    PD = nc.alloc_psum_tensor("PD", [128, 1024], F32).ap()
    HALF = [PA[:, 0:512], PA[:, 512:1024], PBp[:, 0:512], PBp[:, 512:1024]]

    TE, VE, SE, SY, GP = nc.tensor, nc.vector, nc.scalar, nc.sync, nc.gpsimd

    sXT = [Sem(nc, f"sXT{i}") for i in range(2)]
    sXTs = [Sem(nc, f"sXTs{i}") for i in range(3)]
    sW = Sem(nc, "sW")
    sWG = Sem(nc, "sWG")
    sW4g = Sem(nc, "sW4g")
    sW5g = Sem(nc, "sW5g")
    sSRD = [Sem(nc, f"sSRD{i}") for i in range(2)]
    sRDD = Sem(nc, "sRDD")
    sSIND = [Sem(nc, f"sSIND{i}") for i in range(4)]
    sOUTD = [Sem(nc, f"sOUTD{i}") for i in range(2)]

    vStat = Sem(nc, "vStat")
    sRstd = Sem(nc, "sRstd")
    vNmr = Sem(nc, "vNmr")
    sXn = Sem(nc, "sXn")
    tXP = Sem(nc, "tXP")
    vXBe = Sem(nc, "vXBe")
    vGctx = Sem(nc, "vGctx")
    tKQ = Sem(nc, "tKQ")
    vKV = Sem(nc, "vKV")
    gOnes = Sem(nc, "gOnes")
    tSC = Sem(nc, "tSC")
    sExp = Sem(nc, "sExp")
    tAV = Sem(nc, "tAV")
    tAVd = Sem(nc, "tAVd")
    vSR = Sem(nc, "vSR")
    tPair = Sem(nc, "tPair")
    tXP2 = Sem(nc, "tXP2")
    vSMT = Sem(nc, "vSMT")
    vXn2 = Sem(nc, "vXn2")
    tGF = Sem(nc, "tGF")
    vGF = Sem(nc, "vGF")
    sGF = Sem(nc, "sGF")
    vCast = Sem(nc, "vCast")
    tH1 = Sem(nc, "tH1")
    sH1e = Sem(nc, "sH1e")
    tH2 = Sem(nc, "tH2")
    sH2e = Sem(nc, "sH2e")
    tH3 = Sem(nc, "tH3")
    sH3e = Sem(nc, "sH3e")
    tZ = Sem(nc, "tZ")
    sTHR = Sem(nc, "sTHR")
    vCR = Sem(nc, "vCR")
    tCB = Sem(nc, "tCB")
    vMUL = Sem(nc, "vMUL")
    vUPD = Sem(nc, "vUPD")
    sSilu = Sem(nc, "sSilu")
    gTT1 = Sem(nc, "gTT1")
    tRDB = Sem(nc, "tRDB")
    vTT2 = Sem(nc, "vTT2")
    tTRS = Sem(nc, "tTRS")
    vOUT = Sem(nc, "vOUT")

    with nc.Block() as block:
        SY.dma_start(out=identf[:], in_=identf_h[:]).then_inc(sW.h, 16); sW.n += 16
        SY.dma_start(out=onesc[:], in_=onesc_h[:]).then_inc(sW.h, 16); sW.n += 16
        SY.dma_start(out=onesb[:], in_=onesb_h[:]).then_inc(sW.h, 16); sW.n += 16
        SY.dma_start(out=ebias[:], in_=ebias_h[:]).then_inc(sW.h, 16); sW.n += 16
        SY.dma_start(out=sel[:], in_=sel_h[:]).then_inc(sW.h, 16); sW.n += 16
        SY.dma_start(out=cv[:], in_=cv_h[:]).then_inc(sW.h, 16); sW.n += 16
        for src, dst in [(b1c_h, b1c), (b2c_h, b2c), (b3c_h, b3c), (b4_h, b4),
                         (gb1_h, gb1), (gb2_h, gb2), (gb3_h, gb3), (ar_h, ar)]:
            SY.dma_start(out=dst[:], in_=src[:]).then_inc(sW.h, 16); sW.n += 16
        for ft in range(NF):
            GP.dma_start(out=WK[:, ft * 3 * D:(ft + 1) * 3 * D],
                         in_=wkqvT.rearrange("(ft p) j -> ft p j", p=128)[ft]
                         ).then_inc(sWG.h, 16); sWG.n += 16
        GP.dma_start(out=w3sb.rearrange("p (t o) -> p t o", o=256),
                     in_=cw3.rearrange("(t p) o -> p t o", p=128)).then_inc(sWG.h, 16); sWG.n += 16
        GP.dma_start(out=w4sb[:], in_=cw4[:]).then_inc(sWG.h, 16); sWG.n += 16
        GP.dma_start(out=gw3sb[:], in_=gw3[:]).then_inc(sWG.h, 16); sWG.n += 16
        GP.dma_start(out=w1b_s[:], in_=w1r[:]).then_inc(sWG.h, 16); sWG.n += 16
        GP.dma_start(out=w2b_s[:], in_=w2r[:]).then_inc(sWG.h, 16); sWG.n += 16
        W0_TOTAL = sW.n
        WG_TOTAL = sWG.n

        SE.wait_ge(sW.h, W0_TOTAL)
        TE.wait_ge(sW.h, W0_TOTAL)
        VE.wait_ge(sW.h, W0_TOTAL)
        NSLAB = NT // 4
        slab_tok = {}
        lnslot_tok = [0, 0, 0, 0]
        for sl in range(NSLAB):
            sb_ = sl % 3
            if sl >= 3:
                SY.wait_ge(sXn.h, 4 * (sl - 3) + 4)
            SY.dma_start(out=XTs[sb_].rearrange("p (t d) -> p t d", d=D),
                         in_=x_h[4 * sl * 128:(4 * sl + 4) * 128, :]
                         .rearrange("(t p) d -> p t d", p=128)
                         ).then_inc(sXTs[sb_].h, 16)
            sXTs[sb_].n += 16
            slab_tok[sl] = (sb_, sXTs[sb_].n)
        for sl in range(NSLAB):
            sb_ = sl % 3
            VE.wait_ge(sXTs[slab_tok[sl][0]].h, slab_tok[sl][1])
            if sl >= 1:
                VE.wait_ge(sRstd.h, sl)
            for b in range(4):
                xt = XTs[sb_][:, b * D:(b + 1) * D]
                VE.bn_stats(stat[:, 12 * b:12 * b + 6], xt[:, 0:512])
                VE.bn_stats(stat[:, 12 * b + 6:12 * b + 12], xt[:, 512:1024])
            VE.drain()
            for b in range(4):
                inst = VE.bn_aggr(mvs[:, 2 * b:2 * b + 2], stat[:, 12 * b:12 * b + 12])
            vStat.inc(inst)
            SE.wait_ge(vStat.h, vStat.n)
            SE.activation(out=lt4[:],
                          in_=mvs.rearrange("p (t two) -> p t two", two=2)[:, :, 1:2]
                          .rearrange("p t o -> p (t o)"),
                          func=AF.Ln, bias=cv[:, 0:1])
            SE.drain()
            inst = SE.activation(out=rcol[:, 4 * sl:4 * sl + 4], in_=lt4[:], func=AF.Exp, scale=-0.5)
            sRstd.inc(inst)
            VE.wait_ge(sRstd.h, sRstd.n)
            inst = VE.scalar_tensor_tensor(out=nmr[:, 4 * sl:4 * sl + 4],
                                           in0=mvs.rearrange("p (t two) -> p t two", two=2)[:, :, 0:1]
                                           .rearrange("p t o -> p (t o)"),
                                           scalar=-1.0, in1=rcol[:, 4 * sl:4 * sl + 4],
                                           op0=AOP.mult, op1=AOP.mult)
            vNmr.inc(inst)
            SE.wait_ge(vNmr.h, vNmr.n)
            if sl >= 1:
                SE.wait_ge(tXP.h, 8 * (sl - 1) + 8)
            for b in range(4):
                i = 4 * sl + b
                inst = SE.activation(out=XNF[b][:], in_=XTs[sb_][:, b * D:(b + 1) * D],
                                     func=AF.Identity,
                                     bias=nmr[:, i:i + 1], scale=rcol[:, i:i + 1])
            sXn.inc(inst, 4)
            TE.wait_ge(sXn.h, sXn.n)
            for b in range(4):
                i = 4 * sl + b
                for g in range(2):
                    slot = (2 * i + g) % 4
                    if 2 * i + g >= 4:
                        TE.wait_ge(vXBe.h, 2 * i + g - 4 + 1)
                    for k in range(4):
                        ft = 4 * g + k
                        inst = TE.transpose(HALF[slot][:, k * 128:(k + 1) * 128],
                                            XNF[b][:, ft * 128:(ft + 1) * 128], identf[:])
                    tXP.inc(inst)
                    VE.wait_ge(tXP.h, tXP.n)
                    dst = XB.rearrange("p (f t) -> p f t", t=S)[:, 4 * g:4 * g + 4, i * 128:(i + 1) * 128]
                    src = HALF[slot].rearrange("p (f t) -> p f t", t=128)
                    inst = VE.tensor_copy(out=dst, in_=src)
                    vXBe.inc(inst)
                    lnslot_tok[slot] = vXBe.n
        VE.wait_ge(vXBe.h, vXBe.n)
        for ft in range(NF):
            VE.reduce_sum(gct[:, ft:ft + 1], XB[:, ft * S:(ft + 1) * S], axis=mybir.AxisListType.X)
        VE.drain()
        inst = VE.tensor_scalar(out=gctb[:], in0=gct[:], scalar1=1.0 / S, scalar2=None, op0=AOP.mult)
        vGctx.inc(inst)

        TE.wait_ge(sW.h, W0_TOTAL)
        TE.wait_ge(sWG.h, WG_TOTAL)
        GP.wait_ge(vXBe.h, vXBe.n)
        for b2 in range(2):
            inst = GP.memset(v4[b2][:, :, :, 64:65], 1.0)
        gOnes.inc(inst)

        RING = [PA[:, 0:512], PA[:, 512:1024], PC[:, 0:512], PC[:, 512:1024]]
        ring_tok = [(vXBe, lnslot_tok[0]), (vXBe, lnslot_tok[1]), None, None]
        ring_pos = [0]

        def ring_next():
            r = ring_pos[0]
            ring_pos[0] = (r + 1) % 4
            if ring_tok[r] is not None:
                sem, val = ring_tok[r]
                TE.wait_ge(sem.h, val)
            return r

        OUTH = [[PBp[0:65, 0:512], PBp[0:65, 512:1024]],
                [PD[0:65, 0:512], PD[0:65, 512:1024]]]
        out_tok = [(vXBe, max(lnslot_tok[2], lnslot_tok[3])), None]

        kqv_tok = {}
        pair_tok = {}

        def kqv_units(p, b, prologue=False, prev2=None):
            first = [True]

            def gate_ve():
                if first[0]:
                    if prev2 is not None:
                        VE.wait_ge(tAV.h, pair_tok[prev2])
                    first[0] = False

            n0p = N0_PAIR[p]
            units = []
            for fpi, fp in enumerate([8 + p, p]):
                tc0 = (n0p * 128) // 512 if fpi == 1 else 0
                for tc in range(tc0, NTC):
                    units.append(("kq", fpi, fp, tc, 4 * (tc + 1)))
            for tt in range(n0p, NT):
                units.append(("v", 0, 0, tt, tt + 1))
            if prologue:
                units.sort(key=lambda u: (u[4], u[0]))
            for kind, fpi, fp, tc, need in units:
                r = ring_next()
                if prologue:
                    TE.wait_ge(vXBe.h, min(2 * need, 64))
                if kind == "kq":
                    dst = QT[b] if fpi == 0 else KT[b]
                    for ft in range(NF):
                        inst = TE.matmul(RING[r],
                                         WK[:, ft * 3 * D + fp * 128: ft * 3 * D + (fp + 1) * 128],
                                         XB[:, ft * S + tc * 512: ft * S + (tc + 1) * 512],
                                         start=(ft == 0), stop=(ft == NF - 1))
                    tKQ.inc(inst)
                    gate_ve()
                    VE.wait_ge(tKQ.h, tKQ.n)
                    inst = VE.tensor_copy(out=dst[:, tc * 512:(tc + 1) * 512], in_=RING[r])
                else:
                    tt = tc
                    base = 2 * D + p * 128
                    for ft in range(NF):
                        inst = TE.matmul(RING[r][:, 0:128],
                                         XB[:, ft * S + tt * 128: ft * S + (tt + 1) * 128],
                                         WK[:, ft * 3 * D + base: ft * 3 * D + base + 128],
                                         start=(ft == 0), stop=(ft == NF - 1))
                    tKQ.inc(inst)
                    gate_ve()
                    VE.wait_ge(tKQ.h, tKQ.n)
                    inst = VE.tensor_copy(out=v4[b][:, tt, :, 0:64],
                                          in_=RING[r][:, 0:128].rearrange("p (h d) -> p h d", h=2))
                vKV.inc(inst)
                ring_tok[r] = (vKV, vKV.n)
                yield

        SC2 = 512
        NSCP = 4
        pb_av_tok = [0] * 12
        pb_exp_tok = [0] * 12
        srw_tok = [0, 0]
        sridx = [0]

        def attn_units(p, b):
            TE.wait_ge(vKV.h, kqv_tok[p])
            TE.wait_ge(gOnes.h, gOnes.n)
            for scp in range(NSCP):
                started = [False, False]
                jj = [0, 0]
                pend = []

                def emit_attnv(flush=False):
                    while pend:
                        hh, pnt, pj = pend.pop(0)
                        s0 = hh * 6 + (pj % 3) * 2
                        TE.wait_ge(sExp.h, pb_exp_tok[s0 + 1])
                        if not started[hh] and out_tok[hh]:
                            osem, oval = out_tok[hh]
                            TE.wait_ge(osem.h, oval)
                        st = not started[hh]
                        stop = flush and all(x[0] != hh for x in pend)
                        TE.matmul(OUTH[hh][0], v4[b][:, pnt, hh, 0:65],
                                  PB_[s0][:], start=st, stop=stop)
                        inst = TE.matmul(OUTH[hh][1], v4[b][:, pnt, hh, 0:65],
                                         PB_[s0 + 1][:], start=st, stop=stop)
                        started[hh] = True
                        tAV.inc(inst)
                        pb_av_tok[s0] = tAV.n
                        pb_av_tok[s0 + 1] = tAV.n

                for nt in range(N0_PAIR[p], NT):
                    for hh in range(2):
                        h = 2 * p + hh
                        if nt < N0[h]:
                            continue
                        lhs = KT[b][hh * 64:(hh + 1) * 64, nt * 128:(nt + 1) * 128]
                        for sub in range(2):
                            sc = 2 * scp + sub
                            r = ring_next()
                            inst = TE.matmul(RING[r], lhs,
                                             QT[b][hh * 64:(hh + 1) * 64, sc * SC2:(sc + 1) * SC2],
                                             start=True, stop=True)
                            tSC.inc(inst)
                            SE.wait_ge(tSC.h, tSC.n)
                            slot = hh * 6 + (jj[hh] % 3) * 2 + sub
                            if pb_av_tok[slot]:
                                SE.wait_ge(tAV.h, pb_av_tok[slot])
                            inst = SE.activation(out=PB_[slot][:], in_=RING[r], func=AF.Exp,
                                                 bias=ebias[:, h * 32 + nt: h * 32 + nt + 1],
                                                 scale=1.0 / 32.0)
                            sExp.inc(inst)
                            pb_exp_tok[slot] = sExp.n
                            ring_tok[r] = (sExp, sExp.n)
                    emit_attnv()
                    for hh in range(2):
                        if nt >= N0[2 * p + hh]:
                            pend.append((hh, nt, jj[hh]))
                            jj[hh] += 1
                    yield
                emit_attnv(flush=True)
                VE.wait_ge(tAV.h, tAV.n)
                for hh in range(2):
                    h = 2 * p + hh
                    slot = sridx[0] % 2
                    if srw_tok[slot]:
                        VE.wait_ge(sSRD[slot].h, srw_tok[slot])
                    full = [PBp, PD][hh]
                    inst = VE.tensor_copy(out=SRW[slot][0:65, :], in_=full[0:65, :])
                    vSR.inc(inst)
                    out_tok[hh] = (vSR, vSR.n)
                    SY.wait_ge(vSR.h, vSR.n)
                    SY.dma_start(out=sraw_s[h * 64: h * 64 + 64, 2 * scp * SC2:(2 * scp + 2) * SC2],
                                 in_=SRW[slot][0:64, :]).then_inc(sSRD[slot].h, 16)
                    sSRD[slot].n += 16
                    SY.dma_start(out=d_s[h:h + 1, 2 * scp * SC2:(2 * scp + 2) * SC2],
                                 in_=SRW[slot][64:65, :]).then_inc(sSRD[slot].h, 16)
                    sSRD[slot].n += 16
                    srw_tok[slot] = sSRD[slot].n
                    sridx[0] += 1
                yield
            pair_tok[p] = tAV.n

        for _ in kqv_units(0, 0):
            pass
        kqv_tok[0] = vKV.n
        for p in range(8):
            gen_a = attn_units(p, p % 2)
            gen_k = kqv_units(p + 1, (p + 1) % 2) if p < 7 else None
            na = NSCP * (NT - N0_PAIR[p] + 2)
            nk = 36
            acc = 0.0
            rate = nk / max(na, 1)
            for _ in gen_a:
                if gen_k is not None:
                    acc += rate
                    while acc >= 1.0 and gen_k is not None:
                        try:
                            next(gen_k)
                            acc -= 1.0
                        except StopIteration:
                            gen_k = None
            while gen_k is not None:
                try:
                    next(gen_k)
                except StopIteration:
                    gen_k = None
            if p < 7:
                kqv_tok[p + 1] = vKV.n

        GP.wait_ge(tAV.h, tAV.n)
        GP.wait_ge(vSR.h, vSR.n)
        for s4 in sSRD:
            GP.wait_ge(s4.h, s4.n)
        GP.dma_start(out=GCW1.rearrange("p (ft j) -> p ft j", j=512),
                     in_=gw1.rearrange("(ft p) j -> p ft j", p=128)).then_inc(sW4g.h, 16); sW4g.n += 16
        GP.dma_start(out=GCW2.rearrange("p (ft j) -> p ft j", j=256),
                     in_=gw2.rearrange("(ft p) j -> p ft j", p=128)).then_inc(sW4g.h, 16); sW4g.n += 16
        W4_TOTAL = sW4g.n

        SY.wait_ge(tAV.h, tAV.n)
        for s4 in sSRD:
            SY.wait_ge(s4.h, s4.n)
        SE.wait_ge(tAV.h, tAV.n)

        TE.wait_ge(sW4g.h, W4_TOTAL)
        TE.wait_ge(vGctx.h, vGctx.n)
        TE.wait_ge(vKV.h, vKV.n)
        TE.wait_ge(sExp.h, sExp.n)
        TE.wait_ge(vSR.h, vSR.n)
        for ft in range(NF):
            inst = TE.matmul(PC[0:1, 0:512], gctb[:, ft:ft + 1],
                             GCW1[:, ft * 512:(ft + 1) * 512],
                             start=(ft == 0), stop=(ft == NF - 1))
        tGF.inc(inst)
        VE.wait_ge(tGF.h, tGF.n)
        inst = VE.tensor_tensor(out=thr[:], in0=gb1[:], in1=PC[0:1, 0:512], op=AOP.add)
        vGF.inc(inst)
        SE.wait_ge(vGF.h, vGF.n)
        inst = SE.activation(out=g1b[:], in_=thr[:], func=AF.Gelu)
        sGF.inc(inst)
        TE.wait_ge(sGF.h, sGF.n)
        for sg4 in range(4):
            inst = TE.matmul(PD[:, sg4:sg4 + 1], g1b[0:1, sg4 * 128:(sg4 + 1) * 128],
                             onesb[0:1, 0:1], start=True, stop=True)
        tGF.inc(inst)
        VE.wait_ge(tGF.h, tGF.n)
        inst = VE.tensor_copy(out=g1c[:], in_=PD[:, 0:4])
        vGF.inc(inst)
        TE.wait_ge(vGF.h, vGF.n)
        for sg4 in range(4):
            inst = TE.matmul(PC[0:1, 512:768], g1c[:, sg4:sg4 + 1],
                             GCW2[:, sg4 * 256:(sg4 + 1) * 256],
                             start=(sg4 == 0), stop=(sg4 == 3))
        tGF.inc(inst)
        VE.wait_ge(tGF.h, tGF.n)
        inst = VE.tensor_tensor(out=gr2[:], in0=gb2[:], in1=PC[0:1, 512:768], op=AOP.add)
        vGF.inc(inst)
        SE.wait_ge(vGF.h, vGF.n)
        inst = SE.activation(out=g2b[:], in_=gr2[:], func=AF.Gelu)
        sGF.inc(inst)
        TE.wait_ge(sGF.h, sGF.n)
        for sg2 in range(2):
            inst = TE.matmul(PD[:, 4 + sg2:5 + sg2], g2b[0:1, sg2 * 128:(sg2 + 1) * 128],
                             onesb[0:1, 0:1], start=True, stop=True)
        tGF.inc(inst)
        VE.wait_ge(tGF.h, tGF.n)
        inst = VE.tensor_copy(out=g2c[:], in_=PD[:, 4:6])
        vGF.inc(inst)
        TE.wait_ge(vGF.h, vGF.n)
        for sg2 in range(2):
            inst = TE.matmul(PC[0:1, 768:769], g2c[:, sg2:sg2 + 1], gw3sb[:, sg2:sg2 + 1],
                             start=(sg2 == 0), stop=(sg2 == 1))
        tGF.inc(inst)
        VE.tensor_scalar(out=gb3hf[:], in0=gb3[:], scalar1=0.5, scalar2=None, op0=AOP.mult)
        inst = VE.tensor_scalar(out=b4h[:], in0=b4[:], scalar1=0.5, scalar2=None, op0=AOP.mult)
        vGF.inc(inst)
        SE.wait_ge(tGF.h, tGF.n)
        SE.wait_ge(vGF.h, vGF.n)
        inst = SE.activation(out=thg[:], in_=PC[0:1, 768:769], func=AF.Tanh,
                             scale=0.5, bias=gb3hf[0:1, 0:1])
        sGF.inc(inst)
        VE.wait_ge(sGF.h, sGF.n)
        VE.scalar_tensor_tensor(out=bp0[:], in0=thg[:], scalar=0.7, in1=ar[:],
                                op0=AOP.mult, op1=AOP.mult)
        VE.drain()
        VE.tensor_scalar(out=bp[:], in0=bp0[:], scalar1=1.0, scalar2=0.1, op0=AOP.mult, op1=AOP.add)
        inst = VE.tensor_scalar(out=ap_[:], in0=ar[:], scalar1=0.3, scalar2=None, op0=AOP.mult)
        vGF.inc(inst)

        for sl in range(NT // 2):
            sb_ = sl % 2
            if sl >= 2:
                SY.wait_ge(vXn2.h, 2 * (sl - 2) + 2)
            SY.dma_start(out=XT2s[sb_].rearrange("p (t d) -> p t d", d=D),
                         in_=x_h[2 * sl * 128:(2 * sl + 2) * 128, :]
                         .rearrange("(t p) d -> p t d", p=128)
                         ).then_inc(sXT[sb_].h, 16)
            sXT[sb_].n += 16
            VE.wait_ge(sXT[sb_].h, sXT[sb_].n)
            if sl >= 2:
                VE.wait_ge(tXP2.h, 4 * (sl - 2) + 4)
            for b in range(2):
                i = 2 * sl + b
                inst = VE.tensor_scalar(out=XNF2[(2 * sl + b) % 4][:], in0=XT2s[sb_][:, b * D:(b + 1) * D],
                                        scalar1=rcol[:, i:i + 1], scalar2=nmr[:, i:i + 1],
                                        op0=AOP.mult, op1=AOP.add)
            vXn2.inc(inst, 2)
            TE.wait_ge(vXn2.h, vXn2.n)
            for b in range(2):
                i = 2 * sl + b
                for g in range(2):
                    slot = (2 * i + g) % 4
                    if 2 * i + g >= 4:
                        TE.wait_ge(vSMT.h, 2 * i + g - 4 + 1)
                    for k in range(4):
                        ft = 4 * g + k
                        inst = TE.transpose(HALF[slot][:, k * 128:(k + 1) * 128],
                                            XNF2[i % 4][:, ft * 128:(ft + 1) * 128], identf[:])
                    tXP2.inc(inst)
                    VE.wait_ge(tXP2.h, tXP2.n)
                    dst = STATE.rearrange("p (f t) -> p f t", t=S)[:, 4 * g:4 * g + 4, i * 128:(i + 1) * 128]
                    src = HALF[slot].rearrange("p (f t) -> p f t", t=128)
                    inst = VE.tensor_copy(out=dst, in_=src)
                    vSMT.inc(inst)

        GP.wait_ge(vSMT.h, vSMT.n)
        GP.wait_ge(tGF.h, tGF.n)
        GP.wait_ge(sGF.h, sGF.n)
        for s4 in sSRD:
            GP.wait_ge(s4.h, s4.n)
        w1_tok = [0, 0, 0, 0]
        w1_dma_tok = {}
        sW1S = [Sem(nc, f"sW1S{i}") for i in range(4)]
        w1r_r = w1b_s.rearrange("f1 (ft p) j -> f1 p ft j", p=128)
        SY.wait_ge(sWG.h, WG_TOTAL)
        SY.wait_ge(tGF.h, tGF.n)
        SY.wait_ge(sGF.h, sGF.n)

        def w1_prefetch(g1):
            slot = g1 % 4
            if w1_tok[slot]:
                SY.wait_ge(tH1.h, w1_tok[slot])
            SY.dma_start(out=W1S[slot].rearrange("p (ft j) -> p ft j", j=128),
                         in_=w1r_r[g1 % NF]).then_inc(sW1S[slot].h, 16)
            sW1S[slot].n += 16
            w1_dma_tok[g1] = (slot, sW1S[slot].n)
        for g1 in range(4):
            w1_prefetch(g1)
        w2_tok = [0, 0]
        w2_dma_tok = {}
        sW2S = [Sem(nc, f"sW2S{i}") for i in range(2)]
        w2r_r = w2b_s.rearrange("f2 (ft p) j -> f2 p ft j", p=128)

        def w2_prefetch(g2):
            slot = g2 % 2
            if w2_tok[slot]:
                SY.wait_ge(tH2.h, w2_tok[slot])
            SY.dma_start(out=W2S[slot].rearrange("p (ft j) -> p ft j", j=128),
                         in_=w2r_r[g2 % 4]).then_inc(sW2S[slot].h, 16)
            sW2S[slot].n += 16
            w2_dma_tok[g2] = (slot, sW2S[slot].n)
        for g2 in range(2):
            w2_prefetch(g2)

        partner = [4, 5, 6, 7, 0, 1, 2, 3]
        sign = [+1, +1, -1, -1, -1, -1, +1, +1]
        chunk_idx = 0
        h1_g = 0
        h2_g = 0
        h3_g = 0
        for c in range(3):
            for ch in range(NTC):
                col = ch * TC
                def emit_h1(k):
                    nonlocal h1_g
                    TE.wait_ge(vCast.h, k + 1)
                    sbf = SBF2[k % 2]
                    h1buf = H12[k % 2]
                    for f1 in range(NF):
                        slot = h1_g % 2
                        if h1_g >= 2:
                            TE.wait_ge(sH1e.h, h1_g - 2 + 1)
                        wslot, wtok = w1_dma_tok[h1_g]
                        TE.wait_ge(sW1S[wslot].h, wtok)
                        for ft in range(NF):
                            inst = TE.matmul(HALF[slot][:],
                                             W1S[wslot][:, ft * 128:(ft + 1) * 128],
                                             sbf[:, ft * TC:(ft + 1) * TC],
                                             start=(ft == 0), stop=(ft == NF - 1))
                        tH1.inc(inst)
                        w1_tok[wslot] = tH1.n
                        if h1_g + 4 < 24 * NF:
                            w1_prefetch(h1_g + 4)
                        SE.wait_ge(tH1.h, tH1.n)
                        if k > 1 and f1 == 0:
                            SE.wait_ge(tH2.h, (k - 1) * 4)
                        inst = SE.activation(out=h1buf[:, f1 * TC:(f1 + 1) * TC], in_=HALF[slot][:],
                                             func=AF.Gelu, bias=b1c[:, f1:f1 + 1])
                        sH1e.inc(inst)
                        h1_g += 1

                SBF = SBF2[chunk_idx % 2]
                H1 = H12[chunk_idx % 2]
                if chunk_idx == 0:
                    VE.wait_ge(vGF.h, vGF.n)
                    for ft in range(NF):
                        inst = VE.tensor_copy(out=SBF[:, ft * TC:(ft + 1) * TC],
                                              in_=STATE[:, ft * S + col: ft * S + col + TC])
                    vCast.inc(inst)
                    emit_h1(0)
                TE.wait_ge(sH1e.h, NF * (chunk_idx + 1))
                for f2 in range(4):
                    slot = 2 + (h2_g % 2)
                    if h2_g >= 2:
                        TE.wait_ge(sH2e.h, h2_g - 2 + 1)
                    w2slot, w2t = w2_dma_tok[h2_g]
                    TE.wait_ge(sW2S[w2slot].h, w2t)
                    for f1t in range(NF):
                        inst = TE.matmul(HALF[slot][:],
                                         W2S[w2slot][:, f1t * 128:(f1t + 1) * 128],
                                         H1[:, f1t * TC:(f1t + 1) * TC],
                                         start=(f1t == 0), stop=(f1t == NF - 1))
                    tH2.inc(inst)
                    w2_tok[w2slot] = tH2.n
                    if h2_g + 2 < 24 * 4:
                        w2_prefetch(h2_g + 2)
                    SE.wait_ge(tH2.h, tH2.n)
                    if chunk_idx > 0 and f2 == 0:
                        SE.wait_ge(tH3.h, chunk_idx * 2)
                    inst = SE.activation(out=H2[:, f2 * TC:(f2 + 1) * TC], in_=HALF[slot][:],
                                         func=AF.Gelu, bias=b2c[:, f2:f2 + 1])
                    sH2e.inc(inst)
                    h2_g += 1
                TE.wait_ge(sH2e.h, sH2e.n)
                for f3 in range(2):
                    pc_half = PC[:, 0:512] if (h3_g % 2 == 0) else PC[:, 512:1024]
                    if h3_g >= 2:
                        TE.wait_ge(sH3e.h, h3_g - 2 + 1)
                    for f2t in range(4):
                        inst = TE.matmul(pc_half,
                                         w3sb[:, f2t * 256 + f3 * 128: f2t * 256 + (f3 + 1) * 128],
                                         H2[:, f2t * TC:(f2t + 1) * TC],
                                         start=(f2t == 0), stop=(f2t == 3))
                    tH3.inc(inst)
                    SE.wait_ge(tH3.h, tH3.n)
                    if chunk_idx > 0 and f3 == 0:
                        SE.wait_ge(tZ.h, chunk_idx)
                    inst = SE.activation(out=H3[:, f3 * TC:(f3 + 1) * TC], in_=pc_half,
                                         func=AF.Gelu, bias=b3c[:, f3:f3 + 1])
                    sH3e.inc(inst)
                    h3_g += 1
                TE.wait_ge(sH3e.h, sH3e.n)
                if chunk_idx > 0:
                    TE.wait_ge(sTHR.h, chunk_idx)
                for f3t in range(2):
                    inst = TE.matmul(PD[0:1, 0:512], w4sb[:, f3t:f3t + 1],
                                     H3[:, f3t * TC:(f3t + 1) * TC],
                                     start=(f3t == 0), stop=(f3t == 1))
                tZ.inc(inst)
                SE.wait_ge(tZ.h, tZ.n)
                if chunk_idx > 0:
                    SE.wait_ge(vCR.h, chunk_idx)
                inst = SE.activation(out=thr[:], in_=PD[0:1, 0:512], func=AF.Tanh,
                                     scale=0.5, bias=b4h[0:1, 0:1])
                sTHR.inc(inst)
                if chunk_idx + 1 < 24:
                    ncol = ((chunk_idx + 1) % NTC) * TC
                    NSBF = SBF2[(chunk_idx + 1) % 2]
                    VE.wait_ge(tH1.h, max(0, NF * chunk_idx))
                    for ft in range(NF):
                        inst = VE.tensor_copy(out=NSBF[:, ft * TC:(ft + 1) * TC],
                                              in_=STATE[:, ft * S + ncol: ft * S + ncol + TC])
                    vCast.inc(inst)
                VE.wait_ge(sTHR.h, sTHR.n)
                if chunk_idx > 0:
                    VE.wait_ge(tCB.h, chunk_idx)
                inst = VE.tensor_scalar(out=cr[:], in0=thr[:], scalar1=ap_[0:1, 0:1],
                                        scalar2=bp[0:1, 0:1], op0=AOP.mult, op1=AOP.add)
                vCR.inc(inst)
                if chunk_idx + 1 < 24:
                    emit_h1(chunk_idx + 1)
                TE.wait_ge(vCR.h, vCR.n)
                if chunk_idx > 0:
                    TE.wait_ge(vMUL.h, chunk_idx * 8)
                inst = TE.matmul(PD[:, 512:1024], onesc[0:1, :], cr[0:1, :],
                                 start=True, stop=True)
                tCB.inc(inst)
                VE.wait_ge(tCB.h, tCB.n)
                for qtr in range(4):
                    idxs = range(2 * qtr, 2 * qtr + 2)
                    for ii in idxs:
                        jj = partner[ii]
                        inst = VE.tensor_tensor(out=TMP[ii % 2][:],
                                                in0=PD[:, 512:1024],
                                                in1=STATE[:, jj * S + col: jj * S + col + TC],
                                                op=AOP.mult)
                        if ii == 7:
                            vMUL.inc(inst, 8)
                    VE.drain()
                    for ii in idxs:
                        op = AOP.add if sign[ii] > 0 else AOP.subtract
                        inst = VE.tensor_tensor(out=STATE[:, ii * S + col: ii * S + col + TC],
                                                in0=STATE[:, ii * S + col: ii * S + col + TC],
                                                in1=TMP[ii % 2][:], op=op)
                    VE.drain()
                vUPD.inc(inst)
                chunk_idx += 1

        SE.wait_ge(vUPD.h, vUPD.n)
        for ft in range(NF):
            inst = SE.activation(out=STATE[:, ft * S:(ft + 1) * S],
                                 in_=STATE[:, ft * S:(ft + 1) * S], func=AF.Silu)
            sSilu.inc(inst)
        for s4 in sSRD:
            GP.wait_ge(s4.h, s4.n)
        GP.wait_ge(vUPD.h, vUPD.n)
        GP.dma_start(out=RD[:], in_=d_s[:]).then_inc(sRDD.h, 16); sRDD.n += 16
        VE.wait_ge(sRDD.h, sRDD.n)
        VE.wait_ge(vUPD.h, vUPD.n)
        inst = VE.reciprocal(out=RD[:], in_=RD[:])
        vGF.inc(inst)
        RD_READY = vGF.n

        sin_tok = [0, 0]
        xt3_tok = [0, 0]
        out_tok = [0, 0]
        sg_idx = 0
        NTCF = S // TCF
        for tch in range(NTCF):
            sb6 = tch % 2
            if sin_tok[sb6]:
                SE.wait_ge(gTT1.h, sin_tok[sb6])
            if tch == 0:
                SE.wait_ge(vUPD.h, vUPD.n)
            SE.dma_start(out=SINB[sb6].rearrange("p (ft s) -> p ft s", s=TCF),
                         in_=sraw_s[:, tch * TCF:(tch + 1) * TCF]
                         .rearrange("(ft p) s -> p ft s", p=128)
                         ).then_inc(sSIND[sb6].h, 16)
            sSIND[sb6].n += 16
            for k in range(2):
                tt = 2 * tch + k
                if xt3_tok[k]:
                    SY.wait_ge(vOUT.h, xt3_tok[k])
                if tt == 0:
                    SY.wait_ge(vUPD.h, vUPD.n)
                SY.dma_start(out=XT3[k][:], in_=x_h[tt * 128:(tt + 1) * 128, :]
                             ).then_inc(sXT[k].h, 16); sXT[k].n += 16
            pxp = []

            def flush_xp(tch=tch):
                while pxp:
                    ft, tok = pxp.pop(0)
                    TE.wait_ge(vTT2.h, tok)
                    if ft == 0 and tch >= 1:
                        TE.wait_ge(vOUT.h, 2 * tch)
                    for k in range(2):
                        inst = TE.transpose([PC, PD][k][:, ft * 128:(ft + 1) * 128],
                                            SG2[0][:, ft * TCF + k * 128: ft * TCF + (k + 1) * 128],
                                            identf[:])
                    if ft == NF - 1:
                        tTRS.inc(inst, 2)

            for ft in range(NF):
                b2_ = sg_idx % 2
                GP.wait_ge(sSIND[sb6].h, sSIND[sb6].n)
                GP.wait_ge(sSilu.h, ft + 1)
                if sg_idx >= 2:
                    GP.wait_ge(vTT2.h, sg_idx - 2 + 1)
                inst = GP.tensor_tensor(out=FTMP[b2_][:],
                                        in0=STATE[:, ft * S + tch * TCF: ft * S + tch * TCF + TCF],
                                        in1=SINB[sb6][:, ft * TCF:(ft + 1) * TCF], op=AOP.mult)
                gTT1.inc(inst)
                if ft == NF - 1:
                    sin_tok[sb6] = gTT1.n
                slot = sg_idx % 4
                if sg_idx == 0:
                    TE.wait_ge(vUPD.h, vUPD.n)
                    TE.wait_ge(vGF.h, RD_READY)
                if sg_idx >= 4:
                    TE.wait_ge(vTT2.h, sg_idx - 4 + 1)
                inst = TE.matmul(HALF[slot][:, 0:TCF], sel[:, ft * 128:(ft + 1) * 128],
                                 RD[0:16, tch * TCF:(tch + 1) * TCF], start=True, stop=True)
                tRDB.inc(inst)
                flush_xp()
                VE.wait_ge(gTT1.h, gTT1.n)
                VE.wait_ge(tRDB.h, tRDB.n)
                if sg_idx == 0:
                    VE.wait_ge(vGF.h, RD_READY)
                if tch > 0 and ft == 0:
                    VE.wait_ge(tTRS.h, 2 * tch)
                inst = VE.tensor_tensor(out=SG2[0][:, ft * TCF:(ft + 1) * TCF],
                                        in0=FTMP[b2_][:], in1=HALF[slot][:, 0:TCF], op=AOP.mult)
                vTT2.inc(inst)
                pxp.append((ft, vTT2.n))
                sg_idx += 1
            flush_xp()
            for k in range(2):
                tt = 2 * tch + k
                if k == 0:
                    VE.wait_ge(tTRS.h, 2 * (tch + 1))
                VE.wait_ge(sXT[k].h, sXT[k].n)
                if out_tok[k]:
                    VE.wait_ge(sOUTD[k].h, out_tok[k])
                inst = VE.tensor_tensor(out=OUT[k][:], in0=XT3[k][:], in1=[PC, PD][k][:], op=AOP.add)
                vOUT.inc(inst)
                xt3_tok[k] = vOUT.n
                SY.wait_ge(vOUT.h, vOUT.n)
                SY.dma_start(out=out_h[tt * 128:(tt + 1) * 128, :], in_=OUT[k][:]
                             ).then_inc(sOUTD[k].h, 16)
                sOUTD[k].n += 16
                out_tok[k] = sOUTD[k].n
        SY.wait_ge(sOUTD[0].h, sOUTD[0].n)
        SY.wait_ge(sOUTD[1].h, sOUTD[1].n)

    return nc


def _make_host_inputs(inputs):
    x = np.ascontiguousarray(inputs["L0_attn_x"], dtype=np.float32)
    wkqvT = np.ascontiguousarray(inputs["L0_attn_W_kqv"].T)
    common = {
        "wkqvT": wkqvT,
        "w1r": np.ascontiguousarray(inputs["L0_ffn_cn_w1"].reshape(1024, 8, 128).transpose(1, 0, 2)),
        "w2r": np.ascontiguousarray(inputs["L0_ffn_cn_w2"].reshape(1024, 4, 128).transpose(1, 0, 2)),
        "cw3": np.ascontiguousarray(inputs["L0_ffn_cn_w3"]),
        "cw4": np.ascontiguousarray(inputs["L0_ffn_cn_w4"].reshape(2, 128).T),
        "b1c": np.ascontiguousarray(inputs["L0_ffn_cn_b1"].reshape(8, 128).T),
        "b2c": np.ascontiguousarray(inputs["L0_ffn_cn_b2"].reshape(4, 128).T),
        "b3c": np.ascontiguousarray(inputs["L0_ffn_cn_b3"].reshape(2, 128).T),
        "b4": np.ascontiguousarray(inputs["L0_ffn_cn_b4"].reshape(1, 1)),
        "gw1": np.ascontiguousarray(inputs["L0_ffn_gc_w1"]),
        "gw2": np.ascontiguousarray(inputs["L0_ffn_gc_w2"]),
        "gw3": np.ascontiguousarray(inputs["L0_ffn_gc_w3"].reshape(2, 128).T),
        "gb1": np.ascontiguousarray(inputs["L0_ffn_gc_b1"].reshape(1, 512)),
        "gb2": np.ascontiguousarray(inputs["L0_ffn_gc_b2"].reshape(1, 256)),
        "gb3": np.ascontiguousarray(inputs["L0_ffn_gc_b3"].reshape(1, 1)),
        "ar": np.ascontiguousarray(inputs["L0_ffn_adaptive_range"].reshape(1, 1)),
    }
    import ml_dtypes
    common["identf"] = np.eye(128, dtype=np.float32)
    common["onesc"] = np.ones((1, 128), dtype=np.float32)
    common["onesb"] = np.ones((1, 128), dtype=ml_dtypes.bfloat16)
    eb = np.zeros((128, 512), dtype=np.float32)
    for h in range(H):
        for ntt in range(NT):
            p = np.arange(128, dtype=np.float64)
            eb[:, h * 32 + ntt] = (SLOPES[h] * (ntt * 128 + p - (S - 1)) - MARGIN).astype(np.float32)
    common["ebias"] = eb
    selm = np.zeros((16, 1024), dtype=np.float32)
    for p in range(8):
        selm[2 * p, p * 128: p * 128 + 64] = 1.0
        selm[2 * p + 1, p * 128 + 64: (p + 1) * 128] = 1.0
    common["sel"] = selm
    cvt = np.zeros((128, 4), dtype=np.float32)
    cvt[:, 0] = EPS; cvt[:, 1] = -0.5; cvt[:, 2] = 1.0 / 32.0; cvt[:, 3] = 0.5
    common["cv"] = cvt

    in_maps = []
    for b in range(NCORES):
        m = dict(common)
        m["x"] = np.ascontiguousarray(x[b])
        in_maps.append(m)
    return in_maps


def kernel(**inputs):
    in_maps = _make_host_inputs(inputs)
    nc = build_program()
    res = run_bass_kernel_spmd(nc, in_maps, core_ids=list(range(NCORES)))
    out = np.stack([res.results[b]["out"] for b in range(NCORES)], axis=0)
    return out.astype(np.float32)


# version 42
# speedup vs baseline: 1.0184x; 1.0184x over previous
import math
import numpy as np
import concourse.bass as bass
import concourse.mybir as mybir
from concourse.bass_utils import run_bass_kernel_spmd

F32 = mybir.dt.float32
BF16 = mybir.dt.bfloat16
AF = mybir.ActivationFunctionType
AOP = mybir.AluOpType

S, D, H, HD = 4096, 1024, 16, 64
NT = S // 128
NF = D // 128
SC = 1024
NSC = S // SC
TC = 512
NTC = S // TC
EPS = 1e-5
MARGIN = 3.0
SKIP_T = 65.0
NCORES = 8

SLOPES = [2.0 ** (-(h + 1) / 2.0) for h in range(H)]
N0 = [max(0, math.ceil((S - 128 - SKIP_T / SLOPES[h]) / 128.0)) for h in range(H)]
N0_PAIR = [min(N0[2 * p], N0[2 * p + 1]) for p in range(H // 2)]

KIB = 1024


class Sem:
    def __init__(self, nc, name):
        self.h = nc.alloc_semaphore(name)
        self.n = 0

    def inc(self, instr, k=1):
        instr.then_inc(self.h, k)
        self.n += k

    def wait(self, eng, v=None):
        v = self.n if v is None else v
        if v > 0:
            eng.wait_ge(self.h, v)


def build_program(debug=False):
    nc = bass.Bass("TRN2", target_bir_lowering=False)

    x_h = nc.dram_tensor("x", [S, D], F32, kind="ExternalInput").ap()
    wkqvT = nc.dram_tensor("wkqvT", [D, 3 * D], F32, kind="ExternalInput").ap()
    w1r = nc.dram_tensor("w1r", [NF, D, 128], F32, kind="ExternalInput").ap()
    w2r = nc.dram_tensor("w2r", [4, D, 128], F32, kind="ExternalInput").ap()
    cw3 = nc.dram_tensor("cw3", [512, 256], F32, kind="ExternalInput").ap()
    cw4 = nc.dram_tensor("cw4", [128, 2], F32, kind="ExternalInput").ap()
    b1c_h = nc.dram_tensor("b1c", [128, 8], F32, kind="ExternalInput").ap()
    b2c_h = nc.dram_tensor("b2c", [128, 4], F32, kind="ExternalInput").ap()
    b3c_h = nc.dram_tensor("b3c", [128, 2], F32, kind="ExternalInput").ap()
    b4_h = nc.dram_tensor("b4", [1, 1], F32, kind="ExternalInput").ap()
    gw1 = nc.dram_tensor("gw1", [D, 512], F32, kind="ExternalInput").ap()
    gw2 = nc.dram_tensor("gw2", [512, 256], F32, kind="ExternalInput").ap()
    gw3 = nc.dram_tensor("gw3", [128, 2], F32, kind="ExternalInput").ap()
    gb1_h = nc.dram_tensor("gb1", [1, 512], F32, kind="ExternalInput").ap()
    gb2_h = nc.dram_tensor("gb2", [1, 256], F32, kind="ExternalInput").ap()
    gb3_h = nc.dram_tensor("gb3", [1, 1], F32, kind="ExternalInput").ap()
    ar_h = nc.dram_tensor("ar", [1, 1], F32, kind="ExternalInput").ap()
    identf_h = nc.dram_tensor("identf", [128, 128], F32, kind="ExternalInput").ap()
    onesc_h = nc.dram_tensor("onesc", [1, 128], F32, kind="ExternalInput").ap()
    onesb_h = nc.dram_tensor("onesb", [1, 128], BF16, kind="ExternalInput").ap()
    ebias_h = nc.dram_tensor("ebias", [128, 512], F32, kind="ExternalInput").ap()
    sel_h = nc.dram_tensor("sel", [16, 1024], F32, kind="ExternalInput").ap()
    cv_h = nc.dram_tensor("cv", [128, 4], F32, kind="ExternalInput").ap()

    out_h = nc.dram_tensor("out", [S, D], F32, kind="ExternalOutput").ap()

    sk = "ExternalOutput" if debug else "Internal"
    w1b_s = nc.dram_tensor("w1b_s", [NF, D, 128], BF16, kind="Internal").ap()
    w2b_s = nc.dram_tensor("w2b_s", [4, D, 128], BF16, kind="Internal").ap()
    sraw_s = nc.dram_tensor("sraw_s", [D, S], F32, kind=sk).ap()
    d_s = nc.dram_tensor("d_s", [16, S], F32, kind=sk).ap()

    identf = nc.alloc_sbuf_tensor("identf_s", [128, 128], F32).ap()
    onesc = nc.alloc_sbuf_tensor("onesc_s", [1, 128], F32).ap()
    onesb = nc.alloc_sbuf_tensor("onesb_s", [1, 128], BF16).ap()
    ebias = nc.alloc_sbuf_tensor("ebias_s", [128, 512], F32).ap()
    sel = nc.alloc_sbuf_tensor("sel_s", [16, 1024], F32).ap()
    rcol = nc.alloc_sbuf_tensor("rcol", [128, 32], F32).ap()
    nmr = nc.alloc_sbuf_tensor("nmr", [128, 32], F32).ap()
    stat = nc.alloc_sbuf_tensor("stat", [128, 48], F32).ap()
    mvs = nc.alloc_sbuf_tensor("mvs", [128, 8], F32).ap()
    lt4 = nc.alloc_sbuf_tensor("lt4", [128, 4], F32).ap()
    gct = nc.alloc_sbuf_tensor("gct", [128, 8], F32).ap()
    gctb = nc.alloc_sbuf_tensor("gctb", [128, 8], BF16).ap()
    b1c = nc.alloc_sbuf_tensor("b1c_s", [128, 8], F32).ap()
    b2c = nc.alloc_sbuf_tensor("b2c_s", [128, 4], F32).ap()
    b3c = nc.alloc_sbuf_tensor("b3c_s", [128, 2], F32).ap()
    b4 = nc.alloc_sbuf_tensor("b4_s", [1, 1], F32).ap()
    b4h = nc.alloc_sbuf_tensor("b4h", [1, 1], F32).ap()
    gb1 = nc.alloc_sbuf_tensor("gb1_s", [1, 512], F32).ap()
    gb2 = nc.alloc_sbuf_tensor("gb2_s", [1, 256], F32).ap()
    gb3 = nc.alloc_sbuf_tensor("gb3_s", [1, 1], F32).ap()
    gb3hf = nc.alloc_sbuf_tensor("gb3hf", [1, 1], F32).ap()
    ar = nc.alloc_sbuf_tensor("ar_s", [1, 1], F32).ap()
    thg = nc.alloc_sbuf_tensor("thg", [1, 1], F32).ap()
    bp0 = nc.alloc_sbuf_tensor("bp0", [1, 1], F32).ap()
    bp = nc.alloc_sbuf_tensor("bp", [1, 1], F32).ap()
    ap_ = nc.alloc_sbuf_tensor("ap_", [1, 1], F32).ap()
    w3sb = nc.alloc_sbuf_tensor("w3sb", [128, 4 * 256], BF16).ap()
    w4sb = nc.alloc_sbuf_tensor("w4sb", [128, 2], BF16).ap()
    gw3sb = nc.alloc_sbuf_tensor("gw3sb", [128, 2], BF16).ap()
    thr = nc.alloc_sbuf_tensor("thr", [1, 512], F32).ap()
    cr = nc.alloc_sbuf_tensor("cr", [1, 512], F32).ap()
    g1b = nc.alloc_sbuf_tensor("g1b", [1, 512], BF16).ap()
    g1c = nc.alloc_sbuf_tensor("g1c", [128, 4], BF16).ap()
    gr2 = nc.alloc_sbuf_tensor("gr2", [1, 256], F32).ap()
    cv = nc.alloc_sbuf_tensor("cv_s", [128, 4], F32).ap()
    g2b = nc.alloc_sbuf_tensor("g2b", [1, 256], BF16).ap()
    g2c = nc.alloc_sbuf_tensor("g2c", [128, 2], BF16).ap()

    A0 = (nc.sbuf_base + 63) & ~63
    TOP = 229368 - 8
    assert A0 + 186 * KIB <= TOP, f"arena too small: {A0}"

    def at(name, shape, dtype, off_kib):
        t = nc.alloc_sbuf_tensor_at(name, shape, dtype, offset=A0 + int(off_kib * KIB))
        return t.ap()

    XB = at("XB", [128, NF * S], BF16, 0)
    WK = at("WK", [128, NF * 3 * D], BF16, 64)
    XTs = [at(f"XTs{i}", [128, 4 * D], F32, 112 + 16 * i) for i in range(3)]
    XNF = [at(f"XNF{i}", [128, D], F32, 160 + 4 * i) for i in range(4)]
    KT = [at(f"KT{i}", [128, S], BF16, 112 + 8 * i) for i in range(2)]
    QT = [at(f"QT{i}", [128, S], BF16, 128 + 8 * i) for i in range(2)]
    VB = [at(f"VB{i}", [128, 32 * 130], BF16, 144 + 8.25 * i) for i in range(2)]
    PB_ = [at(f"PBUF{i}", [128, 512], BF16, 160.5 + 1 * i) for i in range(12)]
    SRW = [at(f"SRW{i}", [128, 1024], F32, 172.5 + 4 * i) for i in range(2)]
    STATE = at("STATE", [128, NF * S], F32, 0)
    XT2s = [at(f"XT2s{i}", [128, 2 * D], F32, 128 + 8 * i) for i in range(2)]
    XNF2 = [at(f"XNF2{i}", [128, D], F32, 144 + 4 * i) for i in range(4)]
    GCW1 = at("GCW1", [128, NF * 512], BF16, 160)
    GCW2 = at("GCW2", [128, 4 * 256], BF16, 168)
    SBF2 = [at(f"SBF{i}", [128, NF * TC], BF16, 128 + 8 * i) for i in range(2)]
    H12 = [at(f"H1{i}", [128, NF * TC], BF16, 144 + 8 * i) for i in range(2)]
    H2 = at("H2", [128, 4 * TC], BF16, 160)
    H3 = at("H3", [128, 2 * TC], BF16, 164)
    W2S = [at(f"W2S{i}", [128, NF * 128], BF16, 166 + 2 * i) for i in range(2)]
    W1S = [at(f"W1S{i}", [128, NF * 128], BF16, 170 + 2 * i) for i in range(4)]
    TMP = [at(f"TMP{i}", [128, 512], F32, 178 + 2 * i) for i in range(2)]
    TCF = 256
    SINB = [at(f"SINB{i}", [128, NF * TCF], F32, 128 + 8 * i) for i in range(2)]
    FTMP = [at(f"FTMP{i}", [128, TCF], F32, 144 + 1 * i) for i in range(2)]
    RD = at("RD", [16, S], F32, 146)
    XT3 = [at(f"XT3{i}", [128, D], F32, 162 + 4 * i) for i in range(2)]
    OUT = [at(f"OUT{i}", [128, D], F32, 170 + 4 * i) for i in range(2)]
    SG2 = [at(f"SG{i}", [128, NF * TCF], F32, 178) for i in range(1)]

    v4 = [VB[i].rearrange("p (nt h d) -> p nt h d", nt=32, h=2) for i in range(2)]

    PA = nc.alloc_psum_tensor("PA", [128, 1024], F32).ap()
    PBp = nc.alloc_psum_tensor("PBp", [128, 1024], F32).ap()
    PC = nc.alloc_psum_tensor("PC", [128, 1024], F32).ap()
    PD = nc.alloc_psum_tensor("PD", [128, 1024], F32).ap()
    HALF = [PA[:, 0:512], PA[:, 512:1024], PBp[:, 0:512], PBp[:, 512:1024]]

    TE, VE, SE, SY, GP = nc.tensor, nc.vector, nc.scalar, nc.sync, nc.gpsimd

    sXT = [Sem(nc, f"sXT{i}") for i in range(2)]
    sXTs = [Sem(nc, f"sXTs{i}") for i in range(3)]
    sW = Sem(nc, "sW")
    sWG = Sem(nc, "sWG")
    sW4g = Sem(nc, "sW4g")
    sW5g = Sem(nc, "sW5g")
    sSRD = [Sem(nc, f"sSRD{i}") for i in range(2)]
    sRDD = Sem(nc, "sRDD")
    sSIND = [Sem(nc, f"sSIND{i}") for i in range(4)]
    sOUTD = [Sem(nc, f"sOUTD{i}") for i in range(2)]

    vStat = Sem(nc, "vStat")
    sRstd = Sem(nc, "sRstd")
    vNmr = Sem(nc, "vNmr")
    sXn = Sem(nc, "sXn")
    tXP = Sem(nc, "tXP")
    vXBe = Sem(nc, "vXBe")
    vGctx = Sem(nc, "vGctx")
    tKQ = Sem(nc, "tKQ")
    vKV = Sem(nc, "vKV")
    gOnes = Sem(nc, "gOnes")
    tSC = Sem(nc, "tSC")
    sExp = Sem(nc, "sExp")
    tAV = Sem(nc, "tAV")
    tAVd = Sem(nc, "tAVd")
    vSR = Sem(nc, "vSR")
    tPair = Sem(nc, "tPair")
    tXP2 = Sem(nc, "tXP2")
    vSMT = Sem(nc, "vSMT")
    vXn2 = Sem(nc, "vXn2")
    tGF = Sem(nc, "tGF")
    vGF = Sem(nc, "vGF")
    sGF = Sem(nc, "sGF")
    vCast = Sem(nc, "vCast")
    tH1 = Sem(nc, "tH1")
    sH1e = Sem(nc, "sH1e")
    tH2 = Sem(nc, "tH2")
    sH2e = Sem(nc, "sH2e")
    tH3 = Sem(nc, "tH3")
    sH3e = Sem(nc, "sH3e")
    tZ = Sem(nc, "tZ")
    sTHR = Sem(nc, "sTHR")
    vCR = Sem(nc, "vCR")
    tCB = Sem(nc, "tCB")
    vMUL = Sem(nc, "vMUL")
    vUPD = Sem(nc, "vUPD")
    sSilu = Sem(nc, "sSilu")
    gTT1 = Sem(nc, "gTT1")
    tRDB = Sem(nc, "tRDB")
    vTT2 = Sem(nc, "vTT2")
    tTRS = Sem(nc, "tTRS")
    vOUT = Sem(nc, "vOUT")

    with nc.Block() as block:
        SY.dma_start(out=identf[:], in_=identf_h[:]).then_inc(sW.h, 16); sW.n += 16
        SY.dma_start(out=onesc[:], in_=onesc_h[:]).then_inc(sW.h, 16); sW.n += 16
        SY.dma_start(out=onesb[:], in_=onesb_h[:]).then_inc(sW.h, 16); sW.n += 16
        SY.dma_start(out=ebias[:], in_=ebias_h[:]).then_inc(sW.h, 16); sW.n += 16
        SY.dma_start(out=sel[:], in_=sel_h[:]).then_inc(sW.h, 16); sW.n += 16
        SY.dma_start(out=cv[:], in_=cv_h[:]).then_inc(sW.h, 16); sW.n += 16
        for src, dst in [(b1c_h, b1c), (b2c_h, b2c), (b3c_h, b3c), (b4_h, b4),
                         (gb1_h, gb1), (gb2_h, gb2), (gb3_h, gb3), (ar_h, ar)]:
            SY.dma_start(out=dst[:], in_=src[:]).then_inc(sW.h, 16); sW.n += 16
        for ft in range(NF):
            GP.dma_start(out=WK[:, ft * 3 * D:(ft + 1) * 3 * D],
                         in_=wkqvT.rearrange("(ft p) j -> ft p j", p=128)[ft]
                         ).then_inc(sWG.h, 16); sWG.n += 16
        GP.dma_start(out=w3sb.rearrange("p (t o) -> p t o", o=256),
                     in_=cw3.rearrange("(t p) o -> p t o", p=128)).then_inc(sWG.h, 16); sWG.n += 16
        GP.dma_start(out=w4sb[:], in_=cw4[:]).then_inc(sWG.h, 16); sWG.n += 16
        GP.dma_start(out=gw3sb[:], in_=gw3[:]).then_inc(sWG.h, 16); sWG.n += 16
        GP.dma_start(out=w1b_s[:], in_=w1r[:]).then_inc(sWG.h, 16); sWG.n += 16
        GP.dma_start(out=w2b_s[:], in_=w2r[:]).then_inc(sWG.h, 16); sWG.n += 16
        W0_TOTAL = sW.n
        WG_TOTAL = sWG.n

        SE.wait_ge(sW.h, W0_TOTAL)
        TE.wait_ge(sW.h, W0_TOTAL)
        VE.wait_ge(sW.h, W0_TOTAL)
        NSLAB = NT // 4
        slab_tok = {}
        lnslot_tok = [0, 0, 0, 0]
        for sl in range(NSLAB):
            sb_ = sl % 3
            if sl >= 3:
                SY.wait_ge(sXn.h, 4 * (sl - 3) + 4)
            SY.dma_start(out=XTs[sb_].rearrange("p (t d) -> p t d", d=D),
                         in_=x_h[4 * sl * 128:(4 * sl + 4) * 128, :]
                         .rearrange("(t p) d -> p t d", p=128)
                         ).then_inc(sXTs[sb_].h, 16)
            sXTs[sb_].n += 16
            slab_tok[sl] = (sb_, sXTs[sb_].n)
        for sl in range(NSLAB):
            sb_ = sl % 3
            VE.wait_ge(sXTs[slab_tok[sl][0]].h, slab_tok[sl][1])
            if sl >= 1:
                VE.wait_ge(sRstd.h, sl)
            for b in range(4):
                xt = XTs[sb_][:, b * D:(b + 1) * D]
                VE.bn_stats(stat[:, 12 * b:12 * b + 6], xt[:, 0:512])
                VE.bn_stats(stat[:, 12 * b + 6:12 * b + 12], xt[:, 512:1024])
            VE.drain()
            for b in range(4):
                inst = VE.bn_aggr(mvs[:, 2 * b:2 * b + 2], stat[:, 12 * b:12 * b + 12])
            vStat.inc(inst)
            SE.wait_ge(vStat.h, vStat.n)
            SE.activation(out=lt4[:],
                          in_=mvs.rearrange("p (t two) -> p t two", two=2)[:, :, 1:2]
                          .rearrange("p t o -> p (t o)"),
                          func=AF.Ln, bias=cv[:, 0:1])
            SE.drain()
            inst = SE.activation(out=rcol[:, 4 * sl:4 * sl + 4], in_=lt4[:], func=AF.Exp, scale=-0.5)
            sRstd.inc(inst)
            VE.wait_ge(sRstd.h, sRstd.n)
            inst = VE.scalar_tensor_tensor(out=nmr[:, 4 * sl:4 * sl + 4],
                                           in0=mvs.rearrange("p (t two) -> p t two", two=2)[:, :, 0:1]
                                           .rearrange("p t o -> p (t o)"),
                                           scalar=-1.0, in1=rcol[:, 4 * sl:4 * sl + 4],
                                           op0=AOP.mult, op1=AOP.mult)
            vNmr.inc(inst)
            SE.wait_ge(vNmr.h, vNmr.n)
            if sl >= 1:
                SE.wait_ge(tXP.h, 8 * (sl - 1) + 8)
            for b in range(4):
                i = 4 * sl + b
                inst = SE.activation(out=XNF[b][:], in_=XTs[sb_][:, b * D:(b + 1) * D],
                                     func=AF.Identity,
                                     bias=nmr[:, i:i + 1], scale=rcol[:, i:i + 1])
            sXn.inc(inst, 4)
            TE.wait_ge(sXn.h, sXn.n)
            for b in range(4):
                i = 4 * sl + b
                for g in range(2):
                    slot = (2 * i + g) % 4
                    if 2 * i + g >= 4:
                        TE.wait_ge(vXBe.h, 2 * i + g - 4 + 1)
                    for k in range(4):
                        ft = 4 * g + k
                        inst = TE.transpose(HALF[slot][:, k * 128:(k + 1) * 128],
                                            XNF[b][:, ft * 128:(ft + 1) * 128], identf[:])
                    tXP.inc(inst)
                    VE.wait_ge(tXP.h, tXP.n)
                    dst = XB.rearrange("p (f t) -> p f t", t=S)[:, 4 * g:4 * g + 4, i * 128:(i + 1) * 128]
                    src = HALF[slot].rearrange("p (f t) -> p f t", t=128)
                    inst = VE.tensor_copy(out=dst, in_=src)
                    vXBe.inc(inst)
                    lnslot_tok[slot] = vXBe.n
        VE.wait_ge(vXBe.h, vXBe.n)
        for ft in range(NF):
            VE.reduce_sum(gct[:, ft:ft + 1], XB[:, ft * S:(ft + 1) * S], axis=mybir.AxisListType.X)
        VE.drain()
        inst = VE.tensor_scalar(out=gctb[:], in0=gct[:], scalar1=1.0 / S, scalar2=None, op0=AOP.mult)
        vGctx.inc(inst)

        TE.wait_ge(sW.h, W0_TOTAL)
        TE.wait_ge(sWG.h, WG_TOTAL)
        GP.wait_ge(vXBe.h, vXBe.n)
        for b2 in range(2):
            inst = GP.memset(v4[b2][:, :, :, 64:65], 1.0)
        gOnes.inc(inst)

        RING = [PA[:, 0:512], PA[:, 512:1024], PC[:, 0:512], PC[:, 512:1024]]
        ring_tok = [(vXBe, lnslot_tok[0]), (vXBe, lnslot_tok[1]), None, None]
        ring_pos = [0]

        def ring_next():
            r = ring_pos[0]
            ring_pos[0] = (r + 1) % 4
            if ring_tok[r] is not None:
                sem, val = ring_tok[r]
                TE.wait_ge(sem.h, val)
            return r

        OUTH = [[PBp[0:65, 0:512], PBp[0:65, 512:1024]],
                [PD[0:65, 0:512], PD[0:65, 512:1024]]]
        out_tok = [(vXBe, max(lnslot_tok[2], lnslot_tok[3])), None]

        kqv_tok = {}
        pair_tok = {}

        def kqv_units(p, b, prologue=False, prev2=None):
            first = [True]

            def gate_ve():
                if first[0]:
                    if prev2 is not None:
                        VE.wait_ge(tAV.h, pair_tok[prev2])
                    first[0] = False

            n0p = N0_PAIR[p]
            units = []
            for fpi, fp in enumerate([8 + p, p]):
                tc0 = (n0p * 128) // 512 if fpi == 1 else 0
                for tc in range(tc0, NTC):
                    units.append(("kq", fpi, fp, tc, 4 * (tc + 1)))
            for tt in range(n0p, NT):
                units.append(("v", 0, 0, tt, tt + 1))
            if prologue:
                units.sort(key=lambda u: (u[4], u[0]))
            for kind, fpi, fp, tc, need in units:
                r = ring_next()
                if prologue:
                    TE.wait_ge(vXBe.h, min(2 * need, 64))
                if kind == "kq":
                    dst = QT[b] if fpi == 0 else KT[b]
                    for ft in range(NF):
                        inst = TE.matmul(RING[r],
                                         WK[:, ft * 3 * D + fp * 128: ft * 3 * D + (fp + 1) * 128],
                                         XB[:, ft * S + tc * 512: ft * S + (tc + 1) * 512],
                                         start=(ft == 0), stop=(ft == NF - 1))
                    tKQ.inc(inst)
                    gate_ve()
                    VE.wait_ge(tKQ.h, tKQ.n)
                    inst = VE.tensor_copy(out=dst[:, tc * 512:(tc + 1) * 512], in_=RING[r])
                else:
                    tt = tc
                    base = 2 * D + p * 128
                    for ft in range(NF):
                        inst = TE.matmul(RING[r][:, 0:128],
                                         XB[:, ft * S + tt * 128: ft * S + (tt + 1) * 128],
                                         WK[:, ft * 3 * D + base: ft * 3 * D + base + 128],
                                         start=(ft == 0), stop=(ft == NF - 1))
                    tKQ.inc(inst)
                    gate_ve()
                    VE.wait_ge(tKQ.h, tKQ.n)
                    inst = VE.tensor_copy(out=v4[b][:, tt, :, 0:64],
                                          in_=RING[r][:, 0:128].rearrange("p (h d) -> p h d", h=2))
                vKV.inc(inst)
                ring_tok[r] = (vKV, vKV.n)
                yield

        SC2 = 512
        NSCP = 4
        pb_av_tok = [0] * 12
        pb_exp_tok = [0] * 12
        srw_tok = [0, 0]
        sridx = [0]

        def attn_units(p, b):
            TE.wait_ge(vKV.h, kqv_tok[p])
            TE.wait_ge(gOnes.h, gOnes.n)
            for scp in range(NSCP):
                started = [False, False]
                jj = [0, 0]
                pend = []

                def emit_attnv(flush=False):
                    while pend:
                        hh, pnt, pj = pend.pop(0)
                        s0 = hh * 6 + (pj % 3) * 2
                        TE.wait_ge(sExp.h, pb_exp_tok[s0 + 1])
                        if not started[hh] and out_tok[hh]:
                            osem, oval = out_tok[hh]
                            TE.wait_ge(osem.h, oval)
                        st = not started[hh]
                        stop = flush and all(x[0] != hh for x in pend)
                        TE.matmul(OUTH[hh][0], v4[b][:, pnt, hh, 0:65],
                                  PB_[s0][:], start=st, stop=stop)
                        inst = TE.matmul(OUTH[hh][1], v4[b][:, pnt, hh, 0:65],
                                         PB_[s0 + 1][:], start=st, stop=stop)
                        started[hh] = True
                        tAV.inc(inst)
                        pb_av_tok[s0] = tAV.n
                        pb_av_tok[s0 + 1] = tAV.n

                for nt in range(N0_PAIR[p], NT):
                    for hh in range(2):
                        h = 2 * p + hh
                        if nt < N0[h]:
                            continue
                        lhs = KT[b][hh * 64:(hh + 1) * 64, nt * 128:(nt + 1) * 128]
                        for sub in range(2):
                            sc = 2 * scp + sub
                            r = ring_next()
                            inst = TE.matmul(RING[r], lhs,
                                             QT[b][hh * 64:(hh + 1) * 64, sc * SC2:(sc + 1) * SC2],
                                             start=True, stop=True)
                            tSC.inc(inst)
                            SE.wait_ge(tSC.h, tSC.n)
                            slot = hh * 6 + (jj[hh] % 3) * 2 + sub
                            if pb_av_tok[slot]:
                                SE.wait_ge(tAV.h, pb_av_tok[slot])
                            inst = SE.activation(out=PB_[slot][:], in_=RING[r], func=AF.Exp,
                                                 bias=ebias[:, h * 32 + nt: h * 32 + nt + 1],
                                                 scale=1.0 / 32.0)
                            sExp.inc(inst)
                            pb_exp_tok[slot] = sExp.n
                            ring_tok[r] = (sExp, sExp.n)
                    emit_attnv()
                    for hh in range(2):
                        if nt >= N0[2 * p + hh]:
                            pend.append((hh, nt, jj[hh]))
                            jj[hh] += 1
                    yield
                emit_attnv(flush=True)
                VE.wait_ge(tAV.h, tAV.n)
                for hh in range(2):
                    h = 2 * p + hh
                    slot = sridx[0] % 2
                    if srw_tok[slot]:
                        VE.wait_ge(sSRD[slot].h, srw_tok[slot])
                    full = [PBp, PD][hh]
                    inst = VE.tensor_copy(out=SRW[slot][0:65, :], in_=full[0:65, :])
                    vSR.inc(inst)
                    out_tok[hh] = (vSR, vSR.n)
                    SY.wait_ge(vSR.h, vSR.n)
                    SY.dma_start(out=sraw_s[h * 64: h * 64 + 64, 2 * scp * SC2:(2 * scp + 2) * SC2],
                                 in_=SRW[slot][0:64, :]).then_inc(sSRD[slot].h, 16)
                    sSRD[slot].n += 16
                    SY.dma_start(out=d_s[h:h + 1, 2 * scp * SC2:(2 * scp + 2) * SC2],
                                 in_=SRW[slot][64:65, :]).then_inc(sSRD[slot].h, 16)
                    sSRD[slot].n += 16
                    srw_tok[slot] = sSRD[slot].n
                    sridx[0] += 1
                yield
            pair_tok[p] = tAV.n

        for _ in kqv_units(0, 0):
            pass
        kqv_tok[0] = vKV.n
        for p in range(8):
            gen_a = attn_units(p, p % 2)
            gen_k = kqv_units(p + 1, (p + 1) % 2) if p < 7 else None
            na = NSCP * (NT - N0_PAIR[p] + 2)
            nk = 64
            acc = 0.0
            rate = nk / max(na, 1)
            for _ in gen_a:
                if gen_k is not None:
                    acc += rate
                    while acc >= 1.0 and gen_k is not None:
                        try:
                            next(gen_k)
                            acc -= 1.0
                        except StopIteration:
                            gen_k = None
            while gen_k is not None:
                try:
                    next(gen_k)
                except StopIteration:
                    gen_k = None
            if p < 7:
                kqv_tok[p + 1] = vKV.n

        GP.wait_ge(tAV.h, tAV.n)
        GP.wait_ge(vSR.h, vSR.n)
        for s4 in sSRD:
            GP.wait_ge(s4.h, s4.n)
        GP.dma_start(out=GCW1.rearrange("p (ft j) -> p ft j", j=512),
                     in_=gw1.rearrange("(ft p) j -> p ft j", p=128)).then_inc(sW4g.h, 16); sW4g.n += 16
        GP.dma_start(out=GCW2.rearrange("p (ft j) -> p ft j", j=256),
                     in_=gw2.rearrange("(ft p) j -> p ft j", p=128)).then_inc(sW4g.h, 16); sW4g.n += 16
        W4_TOTAL = sW4g.n

        SY.wait_ge(tAV.h, tAV.n)
        for s4 in sSRD:
            SY.wait_ge(s4.h, s4.n)
        SE.wait_ge(tAV.h, tAV.n)

        TE.wait_ge(sW4g.h, W4_TOTAL)
        TE.wait_ge(vGctx.h, vGctx.n)
        TE.wait_ge(vKV.h, vKV.n)
        TE.wait_ge(sExp.h, sExp.n)
        TE.wait_ge(vSR.h, vSR.n)
        for ft in range(NF):
            inst = TE.matmul(PC[0:1, 0:512], gctb[:, ft:ft + 1],
                             GCW1[:, ft * 512:(ft + 1) * 512],
                             start=(ft == 0), stop=(ft == NF - 1))
        tGF.inc(inst)
        VE.wait_ge(tGF.h, tGF.n)
        inst = VE.tensor_tensor(out=thr[:], in0=gb1[:], in1=PC[0:1, 0:512], op=AOP.add)
        vGF.inc(inst)
        SE.wait_ge(vGF.h, vGF.n)
        inst = SE.activation(out=g1b[:], in_=thr[:], func=AF.Gelu)
        sGF.inc(inst)
        TE.wait_ge(sGF.h, sGF.n)
        for sg4 in range(4):
            inst = TE.matmul(PD[:, sg4:sg4 + 1], g1b[0:1, sg4 * 128:(sg4 + 1) * 128],
                             onesb[0:1, 0:1], start=True, stop=True)
        tGF.inc(inst)
        VE.wait_ge(tGF.h, tGF.n)
        inst = VE.tensor_copy(out=g1c[:], in_=PD[:, 0:4])
        vGF.inc(inst)
        TE.wait_ge(vGF.h, vGF.n)
        for sg4 in range(4):
            inst = TE.matmul(PC[0:1, 512:768], g1c[:, sg4:sg4 + 1],
                             GCW2[:, sg4 * 256:(sg4 + 1) * 256],
                             start=(sg4 == 0), stop=(sg4 == 3))
        tGF.inc(inst)
        VE.wait_ge(tGF.h, tGF.n)
        inst = VE.tensor_tensor(out=gr2[:], in0=gb2[:], in1=PC[0:1, 512:768], op=AOP.add)
        vGF.inc(inst)
        SE.wait_ge(vGF.h, vGF.n)
        inst = SE.activation(out=g2b[:], in_=gr2[:], func=AF.Gelu)
        sGF.inc(inst)
        TE.wait_ge(sGF.h, sGF.n)
        for sg2 in range(2):
            inst = TE.matmul(PD[:, 4 + sg2:5 + sg2], g2b[0:1, sg2 * 128:(sg2 + 1) * 128],
                             onesb[0:1, 0:1], start=True, stop=True)
        tGF.inc(inst)
        VE.wait_ge(tGF.h, tGF.n)
        inst = VE.tensor_copy(out=g2c[:], in_=PD[:, 4:6])
        vGF.inc(inst)
        TE.wait_ge(vGF.h, vGF.n)
        for sg2 in range(2):
            inst = TE.matmul(PC[0:1, 768:769], g2c[:, sg2:sg2 + 1], gw3sb[:, sg2:sg2 + 1],
                             start=(sg2 == 0), stop=(sg2 == 1))
        tGF.inc(inst)
        VE.tensor_scalar(out=gb3hf[:], in0=gb3[:], scalar1=0.5, scalar2=None, op0=AOP.mult)
        inst = VE.tensor_scalar(out=b4h[:], in0=b4[:], scalar1=0.5, scalar2=None, op0=AOP.mult)
        vGF.inc(inst)
        SE.wait_ge(tGF.h, tGF.n)
        SE.wait_ge(vGF.h, vGF.n)
        inst = SE.activation(out=thg[:], in_=PC[0:1, 768:769], func=AF.Tanh,
                             scale=0.5, bias=gb3hf[0:1, 0:1])
        sGF.inc(inst)
        VE.wait_ge(sGF.h, sGF.n)
        VE.scalar_tensor_tensor(out=bp0[:], in0=thg[:], scalar=0.7, in1=ar[:],
                                op0=AOP.mult, op1=AOP.mult)
        VE.drain()
        VE.tensor_scalar(out=bp[:], in0=bp0[:], scalar1=1.0, scalar2=0.1, op0=AOP.mult, op1=AOP.add)
        inst = VE.tensor_scalar(out=ap_[:], in0=ar[:], scalar1=0.3, scalar2=None, op0=AOP.mult)
        vGF.inc(inst)

        for sl in range(NT // 2):
            sb_ = sl % 2
            if sl >= 2:
                SY.wait_ge(vXn2.h, 2 * (sl - 2) + 2)
            SY.dma_start(out=XT2s[sb_].rearrange("p (t d) -> p t d", d=D),
                         in_=x_h[2 * sl * 128:(2 * sl + 2) * 128, :]
                         .rearrange("(t p) d -> p t d", p=128)
                         ).then_inc(sXT[sb_].h, 16)
            sXT[sb_].n += 16
            VE.wait_ge(sXT[sb_].h, sXT[sb_].n)
            if sl >= 2:
                VE.wait_ge(tXP2.h, 4 * (sl - 2) + 4)
            for b in range(2):
                i = 2 * sl + b
                inst = VE.tensor_scalar(out=XNF2[(2 * sl + b) % 4][:], in0=XT2s[sb_][:, b * D:(b + 1) * D],
                                        scalar1=rcol[:, i:i + 1], scalar2=nmr[:, i:i + 1],
                                        op0=AOP.mult, op1=AOP.add)
            vXn2.inc(inst, 2)
            TE.wait_ge(vXn2.h, vXn2.n)
            for b in range(2):
                i = 2 * sl + b
                for g in range(2):
                    slot = (2 * i + g) % 4
                    if 2 * i + g >= 4:
                        TE.wait_ge(vSMT.h, 2 * i + g - 4 + 1)
                    for k in range(4):
                        ft = 4 * g + k
                        inst = TE.transpose(HALF[slot][:, k * 128:(k + 1) * 128],
                                            XNF2[i % 4][:, ft * 128:(ft + 1) * 128], identf[:])
                    tXP2.inc(inst)
                    VE.wait_ge(tXP2.h, tXP2.n)
                    dst = STATE.rearrange("p (f t) -> p f t", t=S)[:, 4 * g:4 * g + 4, i * 128:(i + 1) * 128]
                    src = HALF[slot].rearrange("p (f t) -> p f t", t=128)
                    inst = VE.tensor_copy(out=dst, in_=src)
                    vSMT.inc(inst)

        GP.wait_ge(vSMT.h, vSMT.n)
        GP.wait_ge(tGF.h, tGF.n)
        GP.wait_ge(sGF.h, sGF.n)
        for s4 in sSRD:
            GP.wait_ge(s4.h, s4.n)
        w1_tok = [0, 0, 0, 0]
        w1_dma_tok = {}
        sW1S = [Sem(nc, f"sW1S{i}") for i in range(4)]
        w1r_r = w1b_s.rearrange("f1 (ft p) j -> f1 p ft j", p=128)
        SY.wait_ge(sWG.h, WG_TOTAL)
        SY.wait_ge(tGF.h, tGF.n)
        SY.wait_ge(sGF.h, sGF.n)

        def w1_prefetch(g1):
            slot = g1 % 4
            if w1_tok[slot]:
                SY.wait_ge(tH1.h, w1_tok[slot])
            SY.dma_start(out=W1S[slot].rearrange("p (ft j) -> p ft j", j=128),
                         in_=w1r_r[g1 % NF]).then_inc(sW1S[slot].h, 16)
            sW1S[slot].n += 16
            w1_dma_tok[g1] = (slot, sW1S[slot].n)
        for g1 in range(4):
            w1_prefetch(g1)
        w2_tok = [0, 0]
        w2_dma_tok = {}
        sW2S = [Sem(nc, f"sW2S{i}") for i in range(2)]
        w2r_r = w2b_s.rearrange("f2 (ft p) j -> f2 p ft j", p=128)

        def w2_prefetch(g2):
            slot = g2 % 2
            if w2_tok[slot]:
                SY.wait_ge(tH2.h, w2_tok[slot])
            SY.dma_start(out=W2S[slot].rearrange("p (ft j) -> p ft j", j=128),
                         in_=w2r_r[g2 % 4]).then_inc(sW2S[slot].h, 16)
            sW2S[slot].n += 16
            w2_dma_tok[g2] = (slot, sW2S[slot].n)
        for g2 in range(2):
            w2_prefetch(g2)

        partner = [4, 5, 6, 7, 0, 1, 2, 3]
        sign = [+1, +1, -1, -1, -1, -1, +1, +1]
        chunk_idx = 0
        h1_g = 0
        h2_g = 0
        h3_g = 0
        for c in range(3):
            for ch in range(NTC):
                col = ch * TC
                def emit_h1(k):
                    nonlocal h1_g
                    TE.wait_ge(vCast.h, k + 1)
                    sbf = SBF2[k % 2]
                    h1buf = H12[k % 2]
                    for f1 in range(NF):
                        slot = h1_g % 2
                        if h1_g >= 2:
                            TE.wait_ge(sH1e.h, h1_g - 2 + 1)
                        wslot, wtok = w1_dma_tok[h1_g]
                        TE.wait_ge(sW1S[wslot].h, wtok)
                        for ft in range(NF):
                            inst = TE.matmul(HALF[slot][:],
                                             W1S[wslot][:, ft * 128:(ft + 1) * 128],
                                             sbf[:, ft * TC:(ft + 1) * TC],
                                             start=(ft == 0), stop=(ft == NF - 1))
                        tH1.inc(inst)
                        w1_tok[wslot] = tH1.n
                        if h1_g + 4 < 24 * NF:
                            w1_prefetch(h1_g + 4)
                        SE.wait_ge(tH1.h, tH1.n)
                        if k > 1 and f1 == 0:
                            SE.wait_ge(tH2.h, (k - 1) * 4)
                        inst = SE.activation(out=h1buf[:, f1 * TC:(f1 + 1) * TC], in_=HALF[slot][:],
                                             func=AF.Gelu, bias=b1c[:, f1:f1 + 1])
                        sH1e.inc(inst)
                        h1_g += 1

                SBF = SBF2[chunk_idx % 2]
                H1 = H12[chunk_idx % 2]
                if chunk_idx == 0:
                    VE.wait_ge(vGF.h, vGF.n)
                    for ft in range(NF):
                        inst = VE.tensor_copy(out=SBF[:, ft * TC:(ft + 1) * TC],
                                              in_=STATE[:, ft * S + col: ft * S + col + TC])
                    vCast.inc(inst)
                    emit_h1(0)
                TE.wait_ge(sH1e.h, NF * (chunk_idx + 1))
                for f2 in range(4):
                    slot = 2 + (h2_g % 2)
                    if h2_g >= 2:
                        TE.wait_ge(sH2e.h, h2_g - 2 + 1)
                    w2slot, w2t = w2_dma_tok[h2_g]
                    TE.wait_ge(sW2S[w2slot].h, w2t)
                    for f1t in range(NF):
                        inst = TE.matmul(HALF[slot][:],
                                         W2S[w2slot][:, f1t * 128:(f1t + 1) * 128],
                                         H1[:, f1t * TC:(f1t + 1) * TC],
                                         start=(f1t == 0), stop=(f1t == NF - 1))
                    tH2.inc(inst)
                    w2_tok[w2slot] = tH2.n
                    if h2_g + 2 < 24 * 4:
                        w2_prefetch(h2_g + 2)
                    SE.wait_ge(tH2.h, tH2.n)
                    if chunk_idx > 0 and f2 == 0:
                        SE.wait_ge(tH3.h, chunk_idx * 2)
                    inst = SE.activation(out=H2[:, f2 * TC:(f2 + 1) * TC], in_=HALF[slot][:],
                                         func=AF.Gelu, bias=b2c[:, f2:f2 + 1])
                    sH2e.inc(inst)
                    h2_g += 1
                TE.wait_ge(sH2e.h, sH2e.n)
                for f3 in range(2):
                    pc_half = PC[:, 0:512] if (h3_g % 2 == 0) else PC[:, 512:1024]
                    if h3_g >= 2:
                        TE.wait_ge(sH3e.h, h3_g - 2 + 1)
                    for f2t in range(4):
                        inst = TE.matmul(pc_half,
                                         w3sb[:, f2t * 256 + f3 * 128: f2t * 256 + (f3 + 1) * 128],
                                         H2[:, f2t * TC:(f2t + 1) * TC],
                                         start=(f2t == 0), stop=(f2t == 3))
                    tH3.inc(inst)
                    SE.wait_ge(tH3.h, tH3.n)
                    if chunk_idx > 0 and f3 == 0:
                        SE.wait_ge(tZ.h, chunk_idx)
                    inst = SE.activation(out=H3[:, f3 * TC:(f3 + 1) * TC], in_=pc_half,
                                         func=AF.Gelu, bias=b3c[:, f3:f3 + 1])
                    sH3e.inc(inst)
                    h3_g += 1
                TE.wait_ge(sH3e.h, sH3e.n)
                if chunk_idx > 0:
                    TE.wait_ge(sTHR.h, chunk_idx)
                for f3t in range(2):
                    inst = TE.matmul(PD[0:1, 0:512], w4sb[:, f3t:f3t + 1],
                                     H3[:, f3t * TC:(f3t + 1) * TC],
                                     start=(f3t == 0), stop=(f3t == 1))
                tZ.inc(inst)
                SE.wait_ge(tZ.h, tZ.n)
                if chunk_idx > 0:
                    SE.wait_ge(vCR.h, chunk_idx)
                inst = SE.activation(out=thr[:], in_=PD[0:1, 0:512], func=AF.Tanh,
                                     scale=0.5, bias=b4h[0:1, 0:1])
                sTHR.inc(inst)
                if chunk_idx + 1 < 24:
                    ncol = ((chunk_idx + 1) % NTC) * TC
                    NSBF = SBF2[(chunk_idx + 1) % 2]
                    VE.wait_ge(tH1.h, max(0, NF * chunk_idx))
                    for ft in range(NF):
                        inst = VE.tensor_copy(out=NSBF[:, ft * TC:(ft + 1) * TC],
                                              in_=STATE[:, ft * S + ncol: ft * S + ncol + TC])
                    vCast.inc(inst)
                VE.wait_ge(sTHR.h, sTHR.n)
                if chunk_idx > 0:
                    VE.wait_ge(tCB.h, chunk_idx)
                inst = VE.tensor_scalar(out=cr[:], in0=thr[:], scalar1=ap_[0:1, 0:1],
                                        scalar2=bp[0:1, 0:1], op0=AOP.mult, op1=AOP.add)
                vCR.inc(inst)
                if chunk_idx + 1 < 24:
                    emit_h1(chunk_idx + 1)
                TE.wait_ge(vCR.h, vCR.n)
                if chunk_idx > 0:
                    TE.wait_ge(vMUL.h, chunk_idx * 8)
                inst = TE.matmul(PD[:, 512:1024], onesc[0:1, :], cr[0:1, :],
                                 start=True, stop=True)
                tCB.inc(inst)
                VE.wait_ge(tCB.h, tCB.n)
                for qtr in range(4):
                    idxs = range(2 * qtr, 2 * qtr + 2)
                    for ii in idxs:
                        jj = partner[ii]
                        inst = VE.tensor_tensor(out=TMP[ii % 2][:],
                                                in0=PD[:, 512:1024],
                                                in1=STATE[:, jj * S + col: jj * S + col + TC],
                                                op=AOP.mult)
                        if ii == 7:
                            vMUL.inc(inst, 8)
                    VE.drain()
                    for ii in idxs:
                        op = AOP.add if sign[ii] > 0 else AOP.subtract
                        inst = VE.tensor_tensor(out=STATE[:, ii * S + col: ii * S + col + TC],
                                                in0=STATE[:, ii * S + col: ii * S + col + TC],
                                                in1=TMP[ii % 2][:], op=op)
                    VE.drain()
                vUPD.inc(inst)
                chunk_idx += 1

        SE.wait_ge(vUPD.h, vUPD.n)
        for ft in range(NF):
            inst = SE.activation(out=STATE[:, ft * S:(ft + 1) * S],
                                 in_=STATE[:, ft * S:(ft + 1) * S], func=AF.Silu)
            sSilu.inc(inst)
        for s4 in sSRD:
            GP.wait_ge(s4.h, s4.n)
        GP.wait_ge(vUPD.h, vUPD.n)
        GP.dma_start(out=RD[:], in_=d_s[:]).then_inc(sRDD.h, 16); sRDD.n += 16
        VE.wait_ge(sRDD.h, sRDD.n)
        VE.wait_ge(vUPD.h, vUPD.n)
        inst = VE.reciprocal(out=RD[:], in_=RD[:])
        vGF.inc(inst)
        RD_READY = vGF.n

        sin_tok = [0, 0]
        xt3_tok = [0, 0]
        out_tok = [0, 0]
        sg_idx = 0
        NTCF = S // TCF
        for tch in range(NTCF):
            sb6 = tch % 2
            if sin_tok[sb6]:
                SE.wait_ge(gTT1.h, sin_tok[sb6])
            if tch == 0:
                SE.wait_ge(vUPD.h, vUPD.n)
            SE.dma_start(out=SINB[sb6].rearrange("p (ft s) -> p ft s", s=TCF),
                         in_=sraw_s[:, tch * TCF:(tch + 1) * TCF]
                         .rearrange("(ft p) s -> p ft s", p=128)
                         ).then_inc(sSIND[sb6].h, 16)
            sSIND[sb6].n += 16
            for k in range(2):
                tt = 2 * tch + k
                if xt3_tok[k]:
                    SY.wait_ge(vOUT.h, xt3_tok[k])
                if tt == 0:
                    SY.wait_ge(vUPD.h, vUPD.n)
                SY.dma_start(out=XT3[k][:], in_=x_h[tt * 128:(tt + 1) * 128, :]
                             ).then_inc(sXT[k].h, 16); sXT[k].n += 16
            pxp = []

            def flush_xp(tch=tch):
                while pxp:
                    ft, tok = pxp.pop(0)
                    TE.wait_ge(vTT2.h, tok)
                    if ft == 0 and tch >= 1:
                        TE.wait_ge(vOUT.h, 2 * tch)
                    for k in range(2):
                        inst = TE.transpose([PC, PD][k][:, ft * 128:(ft + 1) * 128],
                                            SG2[0][:, ft * TCF + k * 128: ft * TCF + (k + 1) * 128],
                                            identf[:])
                    if ft == NF - 1:
                        tTRS.inc(inst, 2)

            for ft in range(NF):
                b2_ = sg_idx % 2
                GP.wait_ge(sSIND[sb6].h, sSIND[sb6].n)
                GP.wait_ge(sSilu.h, ft + 1)
                if sg_idx >= 2:
                    GP.wait_ge(vTT2.h, sg_idx - 2 + 1)
                inst = GP.tensor_tensor(out=FTMP[b2_][:],
                                        in0=STATE[:, ft * S + tch * TCF: ft * S + tch * TCF + TCF],
                                        in1=SINB[sb6][:, ft * TCF:(ft + 1) * TCF], op=AOP.mult)
                gTT1.inc(inst)
                if ft == NF - 1:
                    sin_tok[sb6] = gTT1.n
                slot = sg_idx % 4
                if sg_idx == 0:
                    TE.wait_ge(vUPD.h, vUPD.n)
                    TE.wait_ge(vGF.h, RD_READY)
                if sg_idx >= 4:
                    TE.wait_ge(vTT2.h, sg_idx - 4 + 1)
                inst = TE.matmul(HALF[slot][:, 0:TCF], sel[:, ft * 128:(ft + 1) * 128],
                                 RD[0:16, tch * TCF:(tch + 1) * TCF], start=True, stop=True)
                tRDB.inc(inst)
                flush_xp()
                VE.wait_ge(gTT1.h, gTT1.n)
                VE.wait_ge(tRDB.h, tRDB.n)
                if sg_idx == 0:
                    VE.wait_ge(vGF.h, RD_READY)
                if tch > 0 and ft == 0:
                    VE.wait_ge(tTRS.h, 2 * tch)
                inst = VE.tensor_tensor(out=SG2[0][:, ft * TCF:(ft + 1) * TCF],
                                        in0=FTMP[b2_][:], in1=HALF[slot][:, 0:TCF], op=AOP.mult)
                vTT2.inc(inst)
                pxp.append((ft, vTT2.n))
                sg_idx += 1
            flush_xp()
            for k in range(2):
                tt = 2 * tch + k
                if k == 0:
                    VE.wait_ge(tTRS.h, 2 * (tch + 1))
                VE.wait_ge(sXT[k].h, sXT[k].n)
                if out_tok[k]:
                    VE.wait_ge(sOUTD[k].h, out_tok[k])
                inst = VE.tensor_tensor(out=OUT[k][:], in0=XT3[k][:], in1=[PC, PD][k][:], op=AOP.add)
                vOUT.inc(inst)
                xt3_tok[k] = vOUT.n
                SY.wait_ge(vOUT.h, vOUT.n)
                SY.dma_start(out=out_h[tt * 128:(tt + 1) * 128, :], in_=OUT[k][:]
                             ).then_inc(sOUTD[k].h, 16)
                sOUTD[k].n += 16
                out_tok[k] = sOUTD[k].n
        SY.wait_ge(sOUTD[0].h, sOUTD[0].n)
        SY.wait_ge(sOUTD[1].h, sOUTD[1].n)

    return nc


def _make_host_inputs(inputs):
    x = np.ascontiguousarray(inputs["L0_attn_x"], dtype=np.float32)
    wkqvT = np.ascontiguousarray(inputs["L0_attn_W_kqv"].T)
    common = {
        "wkqvT": wkqvT,
        "w1r": np.ascontiguousarray(inputs["L0_ffn_cn_w1"].reshape(1024, 8, 128).transpose(1, 0, 2)),
        "w2r": np.ascontiguousarray(inputs["L0_ffn_cn_w2"].reshape(1024, 4, 128).transpose(1, 0, 2)),
        "cw3": np.ascontiguousarray(inputs["L0_ffn_cn_w3"]),
        "cw4": np.ascontiguousarray(inputs["L0_ffn_cn_w4"].reshape(2, 128).T),
        "b1c": np.ascontiguousarray(inputs["L0_ffn_cn_b1"].reshape(8, 128).T),
        "b2c": np.ascontiguousarray(inputs["L0_ffn_cn_b2"].reshape(4, 128).T),
        "b3c": np.ascontiguousarray(inputs["L0_ffn_cn_b3"].reshape(2, 128).T),
        "b4": np.ascontiguousarray(inputs["L0_ffn_cn_b4"].reshape(1, 1)),
        "gw1": np.ascontiguousarray(inputs["L0_ffn_gc_w1"]),
        "gw2": np.ascontiguousarray(inputs["L0_ffn_gc_w2"]),
        "gw3": np.ascontiguousarray(inputs["L0_ffn_gc_w3"].reshape(2, 128).T),
        "gb1": np.ascontiguousarray(inputs["L0_ffn_gc_b1"].reshape(1, 512)),
        "gb2": np.ascontiguousarray(inputs["L0_ffn_gc_b2"].reshape(1, 256)),
        "gb3": np.ascontiguousarray(inputs["L0_ffn_gc_b3"].reshape(1, 1)),
        "ar": np.ascontiguousarray(inputs["L0_ffn_adaptive_range"].reshape(1, 1)),
    }
    import ml_dtypes
    common["identf"] = np.eye(128, dtype=np.float32)
    common["onesc"] = np.ones((1, 128), dtype=np.float32)
    common["onesb"] = np.ones((1, 128), dtype=ml_dtypes.bfloat16)
    eb = np.zeros((128, 512), dtype=np.float32)
    for h in range(H):
        for ntt in range(NT):
            p = np.arange(128, dtype=np.float64)
            eb[:, h * 32 + ntt] = (SLOPES[h] * (ntt * 128 + p - (S - 1)) - MARGIN).astype(np.float32)
    common["ebias"] = eb
    selm = np.zeros((16, 1024), dtype=np.float32)
    for p in range(8):
        selm[2 * p, p * 128: p * 128 + 64] = 1.0
        selm[2 * p + 1, p * 128 + 64: (p + 1) * 128] = 1.0
    common["sel"] = selm
    cvt = np.zeros((128, 4), dtype=np.float32)
    cvt[:, 0] = EPS; cvt[:, 1] = -0.5; cvt[:, 2] = 1.0 / 32.0; cvt[:, 3] = 0.5
    common["cv"] = cvt

    in_maps = []
    for b in range(NCORES):
        m = dict(common)
        m["x"] = np.ascontiguousarray(x[b])
        in_maps.append(m)
    return in_maps


def kernel(**inputs):
    in_maps = _make_host_inputs(inputs)
    nc = build_program()
    res = run_bass_kernel_spmd(nc, in_maps, core_ids=list(range(NCORES)))
    out = np.stack([res.results[b]["out"] for b in range(NCORES)], axis=0)
    return out.astype(np.float32)
